# version 01 (final)
# baseline (speedup 1.0000x reference)
import contextlib

import numpy as np
import ml_dtypes
import concourse.bass as bass
import concourse.mybir as mybir
from concourse.bass_utils import run_bass_kernel_spmd

F32 = mybir.dt.float32
BF16 = mybir.dt.bfloat16
I32 = mybir.dt.int32
AF = mybir.ActivationFunctionType
ALU = mybir.AluOpType
AX = mybir.AxisListType

B, S, DM = 4, 4096, 4096
NH, HD, SSZ, NG, KCONV = 128, 64, 128, 8, 4
INTER = NH * HD
CONV_DIM = INTER + 2 * NG * SSZ
PROJ = INTER + CONV_DIM + NH
CHUNK = 128
TLOC = 2048
LOOK = 3
TEXT = TLOC + LOOK
NCH = TLOC // CHUNK
NKC = DM // 128
NSLAB = PROJ // 128
SLAB_X0, SLAB_B0, SLAB_C0, SLAB_DT, SLAB_G0 = 0, 64, 72, 80, 81
DT_MIN, DT_MAX, EPS_RMS, EPS_LN = 1e-3, 100.0, 1e-5, 1e-5
LORA_SCALE = 2.0

N_CORES = 8

PHASES = ("ph1", "ph2", "ph3a", "hand", "ph3", "ph4", "ph5")
TAPS = ()


ENGS = ("tensor", "vector", "scalar", "gpsimd", "sync")


class Tok:
    __slots__ = ("eng", "idx", "asyncc")

    def __init__(self, eng, idx, asyncc):
        self.eng, self.idx, self.asyncc = eng, idx, asyncc


class Buf:
    __slots__ = ("w", "r", "name")

    def __init__(self, name=""):
        self.w, self.r, self.name = None, [], name


class Sched:
    def __init__(self):
        self.ops = []
        self.nidx = {e: 0 for e in ENGS}
        self.last = {}
        self.fence_toks = []

    def fence(self):
        self.fence_toks = self.barrier_toks()

    def op(self, eng, fn, reads=(), writes=(), deps=(), dma=False,
           sync_after=False, fastout=False):
        d = list(self.fence_toks)
        for b in reads:
            if b.w is not None:
                d.append(b.w)
        for b in writes:
            if b.w is not None:
                d.append(b.w)
            d.extend(b.r)
        d.extend(deps)
        tok = Tok(eng, self.nidx[eng], dma or sync_after)
        self.nidx[eng] += 1
        self.ops.append(dict(eng=eng, fn=fn, deps=d, tok=tok,
                             inc=16 if dma else 1, needed=False,
                             fastout=False and fastout and not (dma or sync_after)))
        for b in reads:
            b.r.append(tok)
        for b in writes:
            b.w = tok
            b.r = []
        self.last[eng] = tok
        return tok

    def barrier_toks(self):
        return [t for t in (self.last.get(e) for e in ENGS) if t is not None]

    def emit(self, block, sems):
        per_eng = {e: [] for e in ENGS}
        for o in self.ops:
            per_eng[o["eng"]].append(o)
        def edge_needed(dep_tok, consumer_eng):
            if dep_tok.eng != consumer_eng:
                return True
            prod = per_eng[dep_tok.eng][dep_tok.idx]
            if dep_tok.eng == "tensor":
                return dep_tok.asyncc
            return dep_tok.asyncc or not prod["fastout"]

        for o in self.ops:
            if o["tok"].asyncc:
                o["needed"] = True
            for t in o["deps"]:
                if edge_needed(t, o["eng"]):
                    per_eng[t.eng][t.idx]["needed"] = True
        cum = {}
        for e in ENGS:
            c = 0
            for o in per_eng[e]:
                if o["needed"]:
                    c += o["inc"]
                cum[(e, o["tok"].idx)] = c
        def mk_section(ename):
            def go(eng):
                hw = {e: 0 for e in ENGS}
                for o in per_eng[ename]:
                    for t in o["deps"]:
                        if not edge_needed(t, ename):
                            continue
                        v = cum[(t.eng, t.idx)]
                        if v > hw[t.eng]:
                            eng.wait_ge(sems[t.eng], v)
                            hw[t.eng] = v
                    inst = o["fn"](eng)
                    if o["needed"]:
                        if inst is None:
                            inst = eng.nop()
                        if o["inc"] == 16 or ename == "tensor":
                            inst.then_inc(sems[ename], o["inc"])
                        else:
                            eng.drain(fusable=True).then_inc(
                                sems[ename], o["inc"])
            return go

        for e in ENGS:
            if per_eng[e]:
                getattr(block, e)(mk_section(e))



def build_program(phases=PHASES, taps=TAPS):
    nc = bass.Bass("TRN2", target_bir_lowering=False)
    sc = Sched()
    out_toks = []

    d_xln = nc.dram_tensor("xln", [TEXT, DM], F32, kind="ExternalInput").ap()
    d_xfm = nc.dram_tensor("xfm", [NKC, 128, TLOC], F32, kind="ExternalInput").ap()
    d_par = nc.dram_tensor("par", [1, 4], I32, kind="ExternalInput").ap()
    d_win = nc.dram_tensor("w_in", [NSLAB, 128, NKC, 128], F32, kind="ExternalInput").ap()
    d_wout = nc.dram_tensor("w_out", [32, 128, 64, 128], F32, kind="ExternalInput").ap()
    d_wbase = nc.dram_tensor("w_base", [32, 128, 32, 128], F32, kind="ExternalInput").ap()
    d_wrout = nc.dram_tensor("w_router", [128, 256], F32, kind="ExternalInput").ap()
    d_loraA = nc.dram_tensor("loraA", [128, NKC, 128], F32, kind="ExternalInput").ap()
    d_loraB = nc.dram_tensor("loraB", [128, 32, 128], F32, kind="ExternalInput").ap()
    d_convw = nc.dram_tensor("convw", [128, 320], F32, kind="ExternalInput").ap()
    d_convb = nc.dram_tensor("convb", [128, 80], F32, kind="ExternalInput").ap()
    d_dtb = nc.dram_tensor("dt_bias", [128, 1], F32, kind="ExternalInput").ap()
    d_acol = nc.dram_tensor("a_col", [128, 1], F32, kind="ExternalInput").ap()
    d_drep = nc.dram_tensor("d_rep", [1, INTER], F32, kind="ExternalInput").ap()
    d_normg = nc.dram_tensor("normw_g", [128, 64], F32, kind="ExternalInput").ap()
    d_nrep = nc.dram_tensor("normw_rep", [1, INTER], F32, kind="ExternalInput").ap()
    d_identb = nc.dram_tensor("ident_b", [128, 128], BF16, kind="ExternalInput").ap()
    d_identf = nc.dram_tensor("ident_f", [128, 128], F32, kind="ExternalInput").ap()
    d_trimask = nc.dram_tensor("trimask", [128, 128], F32, kind="ExternalInput").ap()
    d_eexp = nc.dram_tensor("e_expand", [8, 128], F32, kind="ExternalInput").ap()
    d_ones3 = nc.dram_tensor("ones3", [3, 128], BF16, kind="ExternalInput").ap()

    d_out = nc.dram_tensor("out", [32, 128, TLOC], F32, kind="ExternalOutput").ap()

    d_XNFM = nc.dram_tensor("XNFM", [128, NKC * TEXT], BF16, kind="Internal").ap()
    d_XFMS = nc.dram_tensor("XFMS", [64, 128, TLOC], BF16, kind="Internal").ap()
    d_GFMS = nc.dram_tensor("GFMS", [64, 128, TLOC], BF16, kind="Internal").ap()
    d_BCFM = nc.dram_tensor("BCFM", [16, 128, TLOC], BF16, kind="Internal").ap()
    d_YNTM = nc.dram_tensor("YNTM", [NCH, 128, INTER], BF16, kind="Internal").ap()
    d_SFM = nc.dram_tensor("SFM", [32, 128, TLOC], BF16, kind="Internal").ap()
    d_CSR = nc.dram_tensor("CSR", [3, CHUNK * 128], BF16, kind="Internal").ap()
    d_SHST = nc.dram_tensor("SH_STATE", [128, INTER], F32, kind="Internal",
                            addr_space="Shared").ap()
    d_SHFL = nc.dram_tensor("SH_FLAG", [1, 1], I32, kind="Internal",
                            addr_space="Shared").ap()

    tap_handles = {}
    def tap(name, shape, dtype):
        if name in taps:
            tap_handles[name] = nc.dram_tensor("tap_" + name, shape, dtype,
                                               kind="ExternalOutput").ap()
            return tap_handles[name]
        return None

    with contextlib.ExitStack() as st:
        ent = st.enter_context
        sems = {e: ent(nc.semaphore("S_" + e)) for e in ENGS}

        banks = [ent(nc.psum_tensor(f"bank{i}", [128, 512], F32)) for i in range(8)]

        sb_ident_b = ent(nc.sbuf_tensor("c_identb", [128, 128], BF16))
        sb_ident_f = ent(nc.sbuf_tensor("c_identf", [128, 128], F32))
        sb_trimask = ent(nc.sbuf_tensor("c_trimask", [128, 128], F32))
        sb_convw = ent(nc.sbuf_tensor("c_convw", [128, 320], F32))
        sb_convb = ent(nc.sbuf_tensor("c_convb", [128, 80], F32))
        sb_dtb = ent(nc.sbuf_tensor("c_dtb", [128, 1], F32))
        sb_acol = ent(nc.sbuf_tensor("c_acol", [128, 1], F32))
        sb_normg = ent(nc.sbuf_tensor("c_normg", [128, 64], F32))
        sb_eexp = ent(nc.sbuf_tensor("c_eexp", [8, 128], F32))
        sb_ones3 = ent(nc.sbuf_tensor("c_ones3", [3, 128], BF16))
        sb_par = ent(nc.sbuf_tensor("c_par", [1, 4], I32))

        b_consts = Buf("consts")
        b_bank = [Buf(f"bank{i}") for i in range(8)]

        sb_dtclip = ent(nc.sbuf_tensor("dtclip", [128, TLOC], F32))
        b_dtclip = Buf("dtclip")

        b_XFMS = [Buf(f"XFMS{i}") for i in range(64)]
        b_GFMS = [Buf(f"GFMS{i}") for i in range(64)]
        b_BCFM = [Buf(f"BCFM{i}") for i in range(16)]
        b_XNFMd = Buf("XNFMd")
        b_YNTM = [Buf(f"YNTM{i}") for i in range(NCH)]
        b_SFM = [Buf(f"SFM{i}") for i in range(32)]

        const_pairs = [
            (sb_ident_b, d_identb), (sb_ident_f, d_identf),
            (sb_trimask, d_trimask), (sb_convw, d_convw), (sb_convb, d_convb),
            (sb_dtb, d_dtb), (sb_acol, d_acol), (sb_normg, d_normg),
            (sb_eexp, d_eexp), (sb_ones3, d_ones3), (sb_par, d_par),
        ]
        for sb_c, d_c in const_pairs:
            sc.op("sync",
                  lambda e, sb_c=sb_c, d_c=d_c: e.dma_start(out=sb_c[:], in_=d_c[:]),
                  writes=[b_consts], dma=True)

        with contextlib.ExitStack() as st12:
            e12 = st12.enter_context
            sb_xnfm = e12(nc.sbuf_tensor("xnfm", [128, NKC * TEXT], BF16))
            b_xnfm = Buf("xnfm")
            b_xnfm_cols = [Buf(f"xnfmc{k}") for k in range(NKC)]

            if "ph1" in phases:
                with contextlib.ExitStack() as st1:
                    e1 = st1.enter_context
                    sb_xld = [e1(nc.sbuf_tensor(f"xld{i}", [128, DM], BF16)) for i in range(2)]
                    sb_xnt = [e1(nc.sbuf_tensor(f"xnt{i}", [128, DM], BF16)) for i in range(2)]
                    sb_sq = e1(nc.sbuf_tensor("sq", [128, 8], F32))
                    sb_sqd = e1(nc.sbuf_tensor("sqd", [128, DM], F32))
                    b_xld = [Buf(f"xld{i}") for i in range(2)]
                    b_xnt = [Buf(f"xnt{i}") for i in range(2)]
                    b_sq = Buf("sq")
                    b_sqd = Buf("sqd")
                    b_tpb = b_bank

                    ntile = (TEXT + 127) // 128
                    for i in range(ntile):
                        p = min(128, TEXT - i * 128)
                        bi = i % 2
                        xl, bx = sb_xld[bi], b_xld[bi]
                        xo, bo = sb_xnt[bi], b_xnt[bi]
                        sc.op("gpsimd",
                              lambda e, i=i, p=p, xl=xl: e.dma_start(
                                  out=xl[0:p, :], in_=d_xln[i * 128:i * 128 + p, :]),
                              writes=[bx], dma=True)
                        sc.op("scalar",
                              lambda e, p=p, xl=xl: e.activation(
                                  sb_sqd[0:p, :], xl[0:p, :], AF.Square,
                                  accum_out=sb_sq[0:p, 0:1]),
                              reads=[bx], writes=[b_sqd, b_sq], sync_after=True)
                        sc.op("vector",
                              lambda e, p=p, xl=xl: e.reduce_sum(
                                  sb_sq[0:p, 1:2], xl[0:p, :], axis=AX.X),
                              reads=[bx], writes=[b_sq], sync_after=True)
                        sc.op("vector",
                              lambda e, p=p: e.tensor_scalar(
                                  sb_sq[0:p, 2:4], sb_sq[0:p, 0:2], 1.0 / DM, None,
                                  ALU.mult),
                              reads=[b_sq], writes=[b_sq])
                        sc.op("vector",
                              lambda e, p=p: e.tensor_tensor(
                                  sb_sq[0:p, 4:5], sb_sq[0:p, 3:4], sb_sq[0:p, 3:4],
                                  ALU.mult),
                              reads=[b_sq], writes=[b_sq])
                        sc.op("vector",
                              lambda e, p=p: e.tensor_tensor(
                                  sb_sq[0:p, 5:6], sb_sq[0:p, 2:3], sb_sq[0:p, 4:5],
                                  ALU.subtract),
                              reads=[b_sq], writes=[b_sq])
                        sc.op("vector",
                              lambda e, p=p: e.tensor_scalar(
                                  sb_sq[0:p, 6:7], sb_sq[0:p, 5:6], EPS_LN, None,
                                  ALU.add),
                              reads=[b_sq], writes=[b_sq])
                        sc.op("scalar",
                              lambda e, p=p: e.activation(
                                  sb_sq[0:p, 6:7], sb_sq[0:p, 6:7], AF.Ln),
                              reads=[b_sq], writes=[b_sq])
                        sc.op("scalar",
                              lambda e, p=p: e.activation(
                                  sb_sq[0:p, 6:7], sb_sq[0:p, 6:7], AF.Exp,
                                  scale=-0.5),
                              reads=[b_sq], writes=[b_sq])
                        sc.op("vector",
                              lambda e, p=p, xl=xl, xo=xo: e.tensor_scalar(
                                  xo[0:p, :], xl[0:p, :], sb_sq[0:p, 3:4],
                                  sb_sq[0:p, 6:7], ALU.subtract, ALU.mult),
                              reads=[bx, b_sq], writes=[bo])
                        if i == 0:
                            t_x = tap("xnt0", [128, DM], BF16)
                            if t_x is not None:
                                sc.op("sync",
                                      lambda e, t_x=t_x, xo=xo: e.dma_start(
                                          out=t_x[:], in_=xo[:]),
                                      reads=[bo], dma=True)
                            t_s = tap("sq0", [128, 8], F32)
                            if t_s is not None:
                                sc.op("sync",
                                      lambda e, t_s=t_s: e.dma_start(
                                          out=t_s[:], in_=sb_sq[:]),
                                      reads=[b_sq], dma=True)
                        for j in range(NKC):
                            pb = j % 2
                            bank = banks[pb]
                            bankb = bank[:].bitcast(BF16)
                            t0 = i * 128
                            sc.op("tensor",
                                  lambda e, p=p, j=j, xo=xo, bankb=bankb: e.transpose(
                                      bankb[0:128, 0:p], xo[0:p, j * 128:(j + 1) * 128],
                                      sb_ident_b[0:p, 0:p]),
                                  reads=[bo, b_consts], writes=[b_tpb[pb]])
                            eng = "vector" if (j % 4) < 2 else "scalar"
                            def evac(e, p=p, j=j, t0=t0, bankb=bankb, eng=eng):
                                dst = sb_xnfm[:, j * TEXT + t0: j * TEXT + t0 + p]
                                if eng == "vector":
                                    return e.tensor_copy(dst, bankb[0:128, 0:p])
                                return e.copy(dst, bankb[0:128, 0:p])
                            sc.op(eng, evac, reads=[b_tpb[pb]],
                                  writes=[b_xnfm_cols[j]], deps=[])
                    sc.op("sync",
                          lambda e: e.dma_start(out=d_XNFM[:], in_=sb_xnfm[:]),
                          reads=b_xnfm_cols, writes=[b_XNFMd], dma=True)

                    t = tap("xnfm", [128, NKC * TEXT], BF16)
                    if t is not None:
                        tk = sc.op("sync",
                                   lambda e, t=t: e.dma_start(out=t[:], in_=sb_xnfm[:]),
                                   reads=b_xnfm_cols, dma=True)
                        out_toks.append(tk)

            if "ph2" in phases:
                sc.fence()
                with contextlib.ExitStack() as st2:
                    e2 = st2.enter_context
                    sb_wbuf = [e2(nc.sbuf_tensor(f"wbuf{i}", [128, NKC * 128], BF16))
                               for i in range(2)]
                    sb_sconv = e2(nc.sbuf_tensor("sconv", [128, TEXT], BF16))
                    sb_acc = e2(nc.sbuf_tensor("acc", [128, TLOC], F32))
                    sb_sx = e2(nc.sbuf_tensor("sx", [128, TLOC], BF16))
                    sb_sg = e2(nc.sbuf_tensor("sg", [128, TEXT], BF16))
                    sb_sgn = e2(nc.sbuf_tensor("sgn", [128, TLOC], BF16))
                    sb_dtf = e2(nc.sbuf_tensor("dtf", [128, TLOC], F32))
                    sb_dtt = e2(nc.sbuf_tensor("dtt", [128, TLOC], F32))
                    b_wbuf = [Buf(f"wbuf{i}") for i in range(2)]
                    b_sconv = Buf("sconv")
                    b_acc = Buf("acc")
                    b_sx = Buf("sx")
                    b_sg = Buf("sg")
                    b_sgn = Buf("sgn")
                    b_dtf = Buf("dtf")
                    b_dtt = Buf("dtt")

                    t_xbc = tap("xbc", [80, 128, TLOC], BF16)
                    t_gate = tap("gate", [64, 128, TLOC], BF16)
                    t_dt = tap("dtclip", [128, TLOC], F32)

                    NCHK = [(0, 512), (512, 512), (1024, 512), (1536, 512),
                            (2048, 3)]

                    for s in range(NSLAB):
                        wb, bw = sb_wbuf[s % 2], b_wbuf[s % 2]
                        sc.op("gpsimd",
                              lambda e, s=s, wb=wb: e.dma_start(
                                  out=wb[:], in_=d_win[s].rearrange(
                                      "k kc m -> k (kc m)")),
                              writes=[bw], dma=True)
                        for kc in range(NKC):
                            for ci, (o, n) in enumerate(NCHK):
                                sc.op("tensor",
                                      lambda e, kc=kc, o=o, n=n, ci=ci, wb=wb:
                                      e.matmul(
                                          banks[ci][:, 0:n],
                                          wb[:, kc * 128:(kc + 1) * 128],
                                          sb_xnfm[:, kc * TEXT + o:kc * TEXT + o + n],
                                          start=(kc == 0), stop=(kc == NKC - 1)),
                                      reads=[bw, b_xnfm_cols[kc]],
                                      writes=[b_bank[ci]])
                        if s < SLAB_DT:
                            for ci, (o, n) in enumerate(NCHK):
                                eng = "vector" if ci % 2 == 0 else "scalar"
                                def evc(e, ci=ci, o=o, n=n, eng=eng):
                                    if eng == "vector":
                                        return e.tensor_copy(
                                            sb_sconv[:, o:o + n], banks[ci][:, 0:n])
                                    return e.copy(
                                        sb_sconv[:, o:o + n], banks[ci][:, 0:n])
                                sc.op(eng, evc, reads=[b_bank[ci]], writes=[b_sconv])
                            if s == 0:
                                t_sc = tap("sconv0", [128, TEXT], BF16)
                                if t_sc is not None:
                                    tk = sc.op("sync",
                                               lambda e, t_sc=t_sc: e.dma_start(
                                                   out=t_sc[:], in_=sb_sconv[:]),
                                               reads=[b_sconv], dma=True)
                                    out_toks.append(tk)
                            sc.op("vector",
                                  lambda e, s=s: e.tensor_scalar(
                                      sb_acc[:], sb_sconv[:, 0:TLOC],
                                      sb_convw[:, 4 * s:4 * s + 1], None, ALU.mult),
                                  reads=[b_sconv, b_consts], writes=[b_acc],
                                  fastout=True)
                            for k in range(1, 4):
                                sc.op("vector",
                                      lambda e, s=s, k=k: e.scalar_tensor_tensor(
                                          sb_acc[:], sb_sconv[:, k:k + TLOC],
                                          sb_convw[:, 4 * s + k:4 * s + k + 1],
                                          sb_acc[:], ALU.mult, ALU.add),
                                      reads=[b_sconv, b_consts, b_acc],
                                      writes=[b_acc], fastout=True)
                            sc.op("scalar",
                                  lambda e, s=s: e.activation(
                                      sb_sx[:], sb_acc[:], AF.Silu,
                                      bias=sb_convb[:, s:s + 1]),
                                  reads=[b_acc, b_consts], writes=[b_sx],
                                  fastout=True)
                            if s < SLAB_B0:
                                dst, bd = d_XFMS[s], b_XFMS[s]
                            else:
                                dst, bd = d_BCFM[s - SLAB_B0], b_BCFM[s - SLAB_B0]
                            sc.op("sync",
                                  lambda e, dst=dst: e.dma_start(out=dst[:, :],
                                                                 in_=sb_sx[:]),
                                  reads=[b_sx], writes=[bd], dma=True)
                            if t_xbc is not None:
                                tk = sc.op("sync",
                                           lambda e, s=s: e.dma_start(
                                               out=t_xbc[s], in_=sb_sx[:]),
                                           reads=[b_sx], dma=True)
                                out_toks.append(tk)
                        elif s == SLAB_DT:
                            for ci, (o, n) in enumerate(NCHK):
                                o2 = max(o, LOOK)
                                n2 = n - (o2 - o)
                                if n2 <= 0:
                                    continue
                                sc.op("vector",
                                      lambda e, ci=ci, o=o, o2=o2, n2=n2:
                                      e.tensor_copy(
                                          sb_dtf[:, o2 - LOOK:o2 - LOOK + n2],
                                          banks[ci][:, o2 - o:o2 - o + n2]),
                                      reads=[b_bank[ci]], writes=[b_dtf])
                            sc.op("vector",
                                  lambda e: e.tensor_scalar(
                                      sb_dtf[:], sb_dtf[:], sb_dtb[:, 0:1], None,
                                      ALU.add),
                                  reads=[b_dtf, b_consts], writes=[b_dtf],
                                  fastout=True)
                            sc.op("scalar",
                                  lambda e: e.activation(sb_dtt[:], sb_dtf[:],
                                                         AF.Abs),
                                  reads=[b_dtf], writes=[b_dtt], fastout=True)
                            sc.op("scalar",
                                  lambda e: e.activation(sb_dtt[:], sb_dtt[:],
                                                         AF.Exp, scale=-1.0),
                                  reads=[b_dtt], writes=[b_dtt], fastout=True)
                            sc.op("vector",
                                  lambda e: e.tensor_scalar(
                                      sb_dtt[:], sb_dtt[:], 1.0, None, ALU.add),
                                  reads=[b_dtt], writes=[b_dtt], fastout=True)
                            sc.op("scalar",
                                  lambda e: e.activation(sb_dtt[:], sb_dtt[:],
                                                         AF.Ln),
                                  reads=[b_dtt], writes=[b_dtt], fastout=True)
                            sc.op("vector",
                                  lambda e: e.tensor_scalar(
                                      sb_dtf[:], sb_dtf[:], 0.0, None, ALU.max),
                                  reads=[b_dtf], writes=[b_dtf], fastout=True)
                            sc.op("vector",
                                  lambda e: e.tensor_tensor(
                                      sb_dtclip[:], sb_dtf[:], sb_dtt[:], ALU.add),
                                  reads=[b_dtf, b_dtt], writes=[b_dtclip],
                                  fastout=True)
                            sc.op("vector",
                                  lambda e: e.tensor_scalar(
                                      sb_dtclip[:], sb_dtclip[:], DT_MIN, DT_MAX,
                                      ALU.max, ALU.min),
                                  reads=[b_dtclip], writes=[b_dtclip],
                                  fastout=True)
                            if t_dt is not None:
                                tk = sc.op("sync",
                                           lambda e: e.dma_start(out=t_dt[:],
                                                                 in_=sb_dtclip[:]),
                                           reads=[b_dtclip], dma=True)
                                out_toks.append(tk)
                        else:
                            gi = s - SLAB_G0
                            for ci, (o, n) in enumerate(NCHK):
                                sc.op("scalar",
                                      lambda e, ci=ci, o=o, n=n: e.activation(
                                          sb_sg[:, o:o + n], banks[ci][:, 0:n],
                                          AF.Silu),
                                      reads=[b_bank[ci]], writes=[b_sg],
                                      fastout=True)
                            sc.op("vector",
                                  lambda e: e.tensor_copy(sb_sgn[:],
                                                          sb_sg[:, LOOK:TEXT]),
                                  reads=[b_sg], writes=[b_sgn], fastout=True)
                            sc.op("sync",
                                  lambda e, gi=gi: e.dma_start(out=d_GFMS[gi],
                                                               in_=sb_sgn[:]),
                                  reads=[b_sgn], writes=[b_GFMS[gi]], dma=True)
                            if t_gate is not None:
                                tk = sc.op("sync",
                                           lambda e, gi=gi: e.dma_start(
                                               out=t_gate[gi], in_=sb_sgn[:]),
                                           reads=[b_sgn], dma=True)
                                out_toks.append(tk)

        if "ph3a" in phases or "ph3" in phases:
            sc.fence()
            with contextlib.ExitStack() as st3:
                e3 = st3.enter_context
                sb_srun = e3(nc.sbuf_tensor("srun", [128, INTER], F32))
                sb_xtm = e3(nc.sbuf_tensor("xtm", [128, INTER], BF16))
                sb_gtm = e3(nc.sbuf_tensor("gtm", [128, INTER], BF16))
                sb_ytm = e3(nc.sbuf_tensor("ytm", [128, INTER], BF16))
                sb_bfm = e3(nc.sbuf_tensor("bfm", [128, 1024], BF16))
                sb_cfm = e3(nc.sbuf_tensor("cfm", [128, 1024], BF16))
                sb_btm = e3(nc.sbuf_tensor("btm", [128, 1024], BF16))
                sb_cs3 = e3(nc.sbuf_tensor("cs3", [3, CHUNK * 128], BF16))
                sb_scores = e3(nc.sbuf_tensor("scores", [128, 2048], BF16))
                sb_lmat = e3(nc.sbuf_tensor("lmat", [128, 2048], BF16))
                sb_mrep = e3(nc.sbuf_tensor("mrep", [128, 2048], BF16))
                sb_xdd = e3(nc.sbuf_tensor("xdd", [128, 512], BF16))
                sb_xd8 = e3(nc.sbuf_tensor("xd8", [128, 1024], BF16))
                sb_srun_bf = e3(nc.sbuf_tensor("srun_bf", [128, INTER], BF16))
                sb_t1 = e3(nc.sbuf_tensor("t1", [128, 512], F32))
                sb_dfree = e3(nc.sbuf_tensor("dfree", [128, INTER], BF16))
                sb_nrep = e3(nc.sbuf_tensor("nrep", [128, INTER], BF16))
                sb_z128 = e3(nc.sbuf_tensor("z128", [128, 128], F32))
                sb_pA = e3(nc.sbuf_tensor("pA", [128, 128], F32))
                sb_csHL = e3(nc.sbuf_tensor("csHL", [128, 128], F32))
                sb_s1 = e3(nc.sbuf_tensor("s1", [128, 128], BF16))
                sb_s2 = e3(nc.sbuf_tensor("s2", [128, 128], BF16))
                sb_s3 = e3(nc.sbuf_tensor("s3", [128, 128], BF16))
                sb_r1 = e3(nc.sbuf_tensor("r1", [128, 128], F32))
                sb_r2 = e3(nc.sbuf_tensor("r2", [128, 128], F32))
                sb_lndt = e3(nc.sbuf_tensor("lndt", [128, 128], F32))
                sb_biasHL = e3(nc.sbuf_tensor("biasHL", [128, 128], F32))
                sb_biasLH = e3(nc.sbuf_tensor("biasLH", [128, 128], F32))
                sb_ddtHL = e3(nc.sbuf_tensor("ddtHL", [128, 128], F32))
                sb_ddtLH = e3(nc.sbuf_tensor("ddtLH", [128, 128], F32))
                sb_expcsLH = e3(nc.sbuf_tensor("expcsLH", [128, 128], F32))
                sb_cdbc = e3(nc.sbuf_tensor("cdbc", [128, 128], F32))
                sb_rms = e3(nc.sbuf_tensor("rms", [128, 16], F32))
                sb_flag = e3(nc.sbuf_tensor("flag", [1, 1], I32))

                d_ZERO = nc.dram_tensor("ZEROST", [128, INTER], F32,
                                        kind="Internal").ap()

                b_srun = [Buf(f"srun{h}") for h in range(NH)]
                b_xtm = Buf("xtm3")
                b_gtm = Buf("gtm3")
                b_ytm = Buf("ytm3")
                b_bfm = Buf("bfm3")
                b_cfm = Buf("cfm3")
                b_btm = Buf("btm3")
                b_cs3 = Buf("cs3")
                b_scores = Buf("scores")
                b_lmat = Buf("lmat")
                b_mrep = Buf("mrep")
                b_xdd = Buf("xdd")
                b_xd8 = [Buf("xd8a"), Buf("xd8b")]
                b_srunbf = Buf("srunbf")
                b_t1 = Buf("t1")
                b_dfree = Buf("dfree")
                b_nrep = Buf("nrep")
                b_z128 = Buf("z128")
                b_prep = Buf("prep")
                b_CSR = Buf("CSRd")
                b_rms = Buf("rms")
                b_flag = Buf("flag")
                b_ZERO = Buf("ZEROd")

                hs = e3(nc.semaphore("hs"))
                hp = e3(nc.semaphore("hp"))

                GRP = 16

                def emit_prep(c, full):
                    cs = slice(c * CHUNK, (c + 1) * CHUNK)
                    src = d_XFMS.rearrange("s k t -> (s k) t")[:, cs]
                    sc.op("sync",
                          lambda e, src=src: e.dma_start(out=sb_xtm[:], in_=src,
                                                         transpose=True),
                          reads=b_XFMS, writes=[b_xtm], dma=True)
                    if full:
                        srcg = d_GFMS.rearrange("s k t -> (s k) t")[:, cs]
                        sc.op("sync",
                              lambda e, srcg=srcg: e.dma_start(
                                  out=sb_gtm[:], in_=srcg, transpose=True),
                              reads=b_GFMS, writes=[b_gtm], dma=True)
                        srcc = d_BCFM[8:16, :, cs].rearrange("g k t -> k g t")
                        sc.op("sync",
                              lambda e, srcc=srcc: e.dma_start(
                                  out=sb_cfm[:].rearrange("k (g t) -> k g t", g=8),
                                  in_=srcc),
                              reads=b_BCFM, writes=[b_cfm], dma=True)
                    srcb = d_BCFM[0:8, :, cs].rearrange("g k t -> k g t")
                    sc.op("sync",
                          lambda e, srcb=srcb: e.dma_start(
                              out=sb_bfm[:].rearrange("k (g t) -> k g t", g=8),
                              in_=srcb),
                          reads=b_BCFM, writes=[b_bfm], dma=True)
                    srcbt = d_BCFM[0:8].rearrange("g k t -> (g k) t")[:, cs]
                    sc.op("sync",
                          lambda e, srcbt=srcbt: e.dma_start(out=sb_btm[:],
                                                             in_=srcbt,
                                                             transpose=True),
                          reads=b_BCFM, writes=[b_btm], dma=True)
                    dtv = sb_dtclip[:, cs]
                    sc.op("vector",
                          lambda e, dtv=dtv: e.tensor_scalar(
                              sb_pA[:], dtv, sb_acol[:, 0:1], None, ALU.mult),
                          reads=[b_dtclip, b_consts], writes=[b_prep])
                    sc.op("vector",
                          lambda e: e.tensor_tensor_scan(
                              sb_csHL[:], sb_pA[:], sb_z128[:], 0.0, ALU.add,
                              ALU.add),
                          reads=[b_prep, b_z128], writes=[b_prep],
                          sync_after=True)
                    sc.op("vector", lambda e: e.tensor_copy(sb_s1[:], sb_csHL[:]),
                          reads=[b_prep], writes=[b_prep])
                    sc.op("vector",
                          lambda e: e.tensor_tensor(sb_r1[:], sb_csHL[:], sb_s1[:],
                                                    ALU.subtract),
                          reads=[b_prep], writes=[b_prep])
                    sc.op("vector", lambda e: e.tensor_copy(sb_s2[:], sb_r1[:]),
                          reads=[b_prep], writes=[b_prep])
                    sc.op("vector",
                          lambda e: e.tensor_tensor(sb_r2[:], sb_r1[:], sb_s2[:],
                                                    ALU.subtract),
                          reads=[b_prep], writes=[b_prep])
                    sc.op("vector", lambda e: e.tensor_copy(sb_s3[:], sb_r2[:]),
                          reads=[b_prep], writes=[b_prep])
                    for i, t in enumerate((sb_s1, sb_s2, sb_s3)):
                        sc.op("sync",
                              lambda e, i=i, t=t: e.dma_start(
                                  out=d_CSR[i].rearrange("(p n) -> p n", p=128),
                                  in_=t[:]),
                              reads=[b_prep], writes=[b_CSR], dma=True)
                    sc.op("sync",
                          lambda e: e.dma_start(out=sb_cs3[0:3, :], in_=d_CSR[:]),
                          reads=[b_CSR], writes=[b_cs3], dma=True)
                    sc.op("tensor",
                          lambda e: e.matmul(banks[5][:, 0:128], sb_ones3[:],
                                             sb_cs3[0:3, 127::128], start=True,
                                             stop=True),
                          reads=[b_cs3, b_consts], writes=[b_bank[5]])
                    sc.op("scalar",
                          lambda e: e.activation(sb_cdbc[:], banks[5][:, 0:128],
                                                 AF.Exp),
                          reads=[b_bank[5]], writes=[b_prep])
                    sc.op("vector",
                          lambda e: e.tensor_scalar(
                              sb_ddtHL[:], sb_csHL[:], sb_csHL[:, 127:128], None,
                              ALU.subtract),
                          reads=[b_prep], writes=[b_prep])
                    sc.op("scalar",
                          lambda e: e.activation(sb_ddtHL[:], sb_ddtHL[:], AF.Exp,
                                                 scale=-1.0),
                          reads=[b_prep], writes=[b_prep])
                    sc.op("vector",
                          lambda e, dtv=dtv: e.tensor_tensor(
                              sb_ddtHL[:], sb_ddtHL[:], dtv, ALU.mult),
                          reads=[b_prep, b_dtclip], writes=[b_prep])
                    sc.op("tensor",
                          lambda e: e.transpose(banks[6][:, 0:128], sb_ddtHL[:],
                                                sb_ident_f[:]),
                          reads=[b_prep, b_consts], writes=[b_bank[6]])
                    sc.op("vector",
                          lambda e: e.tensor_copy(sb_ddtLH[:], banks[6][:, 0:128]),
                          reads=[b_bank[6]], writes=[b_prep])
                    if full:
                        sc.op("scalar",
                              lambda e, dtv=dtv: e.activation(sb_lndt[:], dtv,
                                                              AF.Ln),
                              reads=[b_dtclip], writes=[b_prep])
                        sc.op("vector",
                              lambda e: e.tensor_tensor(
                                  sb_biasHL[:], sb_lndt[:], sb_csHL[:],
                                  ALU.subtract),
                              reads=[b_prep], writes=[b_prep])
                        sc.op("tensor",
                              lambda e: e.transpose(banks[6][:, 0:128],
                                                    sb_biasHL[:], sb_ident_f[:]),
                              reads=[b_prep, b_consts], writes=[b_bank[6]])
                        sc.op("vector",
                              lambda e: e.tensor_copy(sb_biasLH[:],
                                                      banks[6][:, 0:128]),
                              reads=[b_bank[6]], writes=[b_prep])
                        sc.op("tensor",
                              lambda e: e.transpose(banks[6][:, 0:128], sb_csHL[:],
                                                    sb_ident_f[:]),
                              reads=[b_prep, b_consts], writes=[b_bank[6]])
                        sc.op("scalar",
                              lambda e: e.activation(sb_expcsLH[:],
                                                     banks[6][:, 0:128], AF.Exp),
                              reads=[b_bank[6]], writes=[b_prep])

                def emit_states_batch(c, j, full):
                    h0 = 8 * j
                    for i in range(8):
                        h = h0 + i
                        sc.op("gpsimd",
                              lambda e, i=i, h=h: e.tensor_scalar(
                                  sb_xdd[:, i * 64:(i + 1) * 64],
                                  sb_xtm[:, h * 64:(h + 1) * 64],
                                  sb_ddtLH[:, h:h + 1], None, ALU.mult),
                              reads=[b_xtm, b_prep], writes=[b_xdd])
                    sbank = 4 if not full else state_bank(j)
                    for i in range(8):
                        h = h0 + i
                        g = h // GRP
                        sc.op("tensor",
                              lambda e, i=i, g=g, sbank=sbank: e.matmul(
                                  banks[sbank][:, i * 64:(i + 1) * 64],
                                  sb_btm[:, g * 128:(g + 1) * 128],
                                  sb_xdd[:, i * 64:(i + 1) * 64],
                                  start=(i == 0), stop=(i == 7)),
                              reads=[b_btm, b_xdd], writes=[b_bank[sbank]])
                    if full:
                        for half in range(2):
                            hh = h0 + 4 * half
                            bk = half
                            sc.op("tensor",
                                  lambda e, hh=hh, bk=bk: e.matmul(
                                      banks[bk][:, 0:512], sb_ones3[:],
                                      sb_cs3[0:3, hh * 128:(hh + 4) * 128],
                                      start=True, stop=True),
                                  reads=[b_cs3, b_consts], writes=[b_bank[bk]])
                            for q in range(4):
                                h = hh + q
                                sc.op("scalar",
                                      lambda e, h=h, q=q, bk=bk: e.activation(
                                          sb_lmat[:, (h % GRP) * 128:
                                                  (h % GRP + 1) * 128],
                                          banks[bk][:, q * 128:(q + 1) * 128],
                                          AF.Exp, bias=sb_biasLH[:, h:h + 1]),
                                      reads=[b_bank[bk], b_prep],
                                      writes=[b_lmat])
                        if j % 2 == 1:
                            g = h0 // GRP
                            sc.op("tensor",
                                  lambda e, g=g: e.matmul(
                                      banks[5][:, 0:128],
                                      sb_bfm[:, g * 128:(g + 1) * 128],
                                      sb_cfm[:, g * 128:(g + 1) * 128],
                                      start=True, stop=True),
                                  reads=[b_bfm, b_cfm], writes=[b_bank[5]])
                            sc.op("vector",
                                  lambda e: e.tensor_tensor(
                                      sb_mrep[:, 0:128], banks[5][:, 0:128],
                                      sb_trimask[:], ALU.mult),
                                  reads=[b_bank[5], b_consts], writes=[b_mrep])
                            for w in (128, 256, 512, 1024):
                                sc.op("vector",
                                      lambda e, w=w: e.tensor_copy(
                                          sb_mrep[:, w:2 * w], sb_mrep[:, 0:w]),
                                      reads=[b_mrep], writes=[b_mrep],
                                      fastout=(w >= 256))
                            sc.op("vector",
                                  lambda e: e.scalar_tensor_tensor(
                                      sb_scores[:], sb_lmat[:], 3.0e38,
                                      sb_mrep[:], ALU.min, ALU.mult),
                                  reads=[b_lmat, b_mrep], writes=[b_scores],
                                  fastout=True)
                        xo = (j % 2) * 512
                        sc.op("gpsimd",
                              lambda e, h0=h0, xo=xo: e.tensor_tensor(
                                  sb_xd8[:, xo:xo + 512],
                                  sb_xtm[:, h0 * 64:(h0 + 8) * 64],
                                  sb_dfree[:, h0 * 64:(h0 + 8) * 64], ALU.mult),
                              reads=[b_xtm, b_dfree], writes=[b_xd8[j % 2]])
                    if full and j % 2 == 1:
                        for jj in (j - 1, j):
                            emit_y_batch(c, jj)
                    if not full:
                        for i in range(8):
                            h = h0 + i
                            sc.op("vector",
                                  lambda e, i=i, h=h: e.scalar_tensor_tensor(
                                      sb_srun[:, h * 64:(h + 1) * 64],
                                      sb_srun[:, h * 64:(h + 1) * 64],
                                      sb_cdbc[:, h:h + 1],
                                      banks[4][:, i * 64:(i + 1) * 64],
                                      ALU.mult, ALU.add),
                                  reads=[b_srun[h], b_prep, b_bank[4]],
                                  writes=[b_srun[h]])

                def state_bank(j):
                    return 4 if j % 2 == 0 else 7

                def emit_y_batch(c, j):
                    h0 = 8 * j
                    sbank = state_bank(j)
                    for i in range(8):
                        h = h0 + i
                        sc.op("tensor",
                              lambda e, i=i, h=h: e.matmul(
                                  banks[2][:, i * 64:(i + 1) * 64],
                                  sb_scores[:, (h % GRP) * 128:(h % GRP + 1) * 128],
                                  sb_xtm[:, h * 64:(h + 1) * 64],
                                  start=(i == 0), stop=False),
                              reads=[b_scores, b_xtm], writes=[b_bank[2]])
                    xo = (j % 2) * 512
                    sc.op("tensor",
                          lambda e, xo=xo: e.matmul(
                              banks[2][:, 0:512], sb_ident_b[:],
                              sb_xd8[:, xo:xo + 512], start=False, stop=True),
                          reads=[b_xd8[j % 2], b_consts], writes=[b_bank[2]])
                    for i in range(8):
                        h = h0 + i
                        g = h // GRP
                        sc.op("tensor",
                              lambda e, i=i, h=h, g=g: e.matmul(
                                  banks[3][:, i * 64:(i + 1) * 64],
                                  sb_cfm[:, g * 128:(g + 1) * 128],
                                  sb_srun_bf[:, h * 64:(h + 1) * 64],
                                  start=(i == 0), stop=(i == 7)),
                              reads=[b_cfm, b_srunbf], writes=[b_bank[3]])
                    for i in range(8):
                        h = h0 + i
                        sc.op("vector",
                              lambda e, i=i, h=h, sbank=sbank:
                              e.scalar_tensor_tensor(
                                  sb_srun[:, h * 64:(h + 1) * 64],
                                  sb_srun[:, h * 64:(h + 1) * 64],
                                  sb_cdbc[:, h:h + 1],
                                  banks[sbank][:, i * 64:(i + 1) * 64],
                                  ALU.mult, ALU.add),
                              reads=[b_srun[h], b_prep, b_bank[sbank]],
                              writes=[b_srun[h]])
                    e8 = sb_expcsLH[:, h0:h0 + 8].unsqueeze(2).broadcast_to(
                        [128, 8, 64])
                    sc.op("vector",
                          lambda e, e8=e8: e.tensor_tensor(
                              sb_t1[:].rearrange("p (h q) -> p h q", h=8),
                              banks[3][:].rearrange("p (h q) -> p h q", h=8),
                              e8, ALU.mult),
                          reads=[b_bank[3], b_prep], writes=[b_t1], fastout=True)
                    sc.op("vector",
                          lambda e, h0=h0: e.tensor_tensor(
                              sb_ytm[:, h0 * 64:(h0 + 8) * 64], banks[2][:],
                              sb_t1[:], ALU.add),
                          reads=[b_bank[2], b_t1], writes=[b_ytm], fastout=True)

                if "ph3a" in phases:
                    sc.op("vector", lambda e: e.memset(sb_z128[:], 0.0),
                          writes=[b_z128])
                    sc.op("vector", lambda e: e.memset(sb_srun[:], 0.0),
                          writes=list(b_srun), fastout=True)
                    sc.op("sync",
                          lambda e: e.dma_start(out=d_ZERO[:], in_=sb_srun[:]),
                          reads=list(b_srun), writes=[b_ZERO], dma=True)
                    for c in range(NCH):
                        emit_prep(c, full=False)
                        for j in range(16):
                            emit_states_batch(c, j, full=False)

                if "hand" in phases:
                    def hand_fn(e):
                        with e.register() as rpar, e.register() as rnonce, \
                                e.register() as rflag, e.register() as rcnt:
                            e.reg_load(rpar, sb_par[0:1, 0:1])
                            e.reg_load(rnonce, sb_par[0:1, 1:2])
                            e.reg_mov(rcnt, 16)
                            with e.If_eq(rpar, 0):
                                e.dma_start(out=d_SHST[:], in_=sb_srun[:]
                                            ).then_inc(hs, 16)
                                e.wait_ge(hs, 16)
                                e.store(sb_flag[0:1, 0:1], rnonce)
                                e.dma_start(out=d_SHFL[:], in_=sb_flag[:]
                                            ).then_inc(hs, 16)
                                e.wait_ge(hs, 32)
                                e.dma_start(out=sb_srun[:], in_=d_ZERO[:]
                                            ).then_inc(hs, 16)
                                e.wait_ge(hs, 48)
                            with e.Else():
                                e.reg_mov(rflag, 0)

                                def cond():
                                    e.dma_start(out=sb_flag[:], in_=d_SHFL[:]
                                                ).then_inc(hp, 16)
                                    e.wait_ge(hp, rcnt)
                                    e.reg_add(rcnt, rcnt, 16)
                                    e.reg_load(rflag, sb_flag[0:1, 0:1])
                                    return (e.compute_val(rflag) !=
                                            e.compute_val(rnonce))

                                with e.While(cond):
                                    e.nop(cycle_cnt=2000)
                                e.dma_start(out=sb_srun[:], in_=d_SHST[:]
                                            ).then_inc(hs, 16)
                                e.wait_ge(hs, 16)
                        return e.nop()
                    sc.op("sync", hand_fn,
                          reads=list(b_srun) + [b_consts, b_ZERO],
                          writes=list(b_srun) + [b_flag])

                if "ph3" in phases:
                    if "ph3a" not in phases:
                        sc.op("vector", lambda e: e.memset(sb_z128[:], 0.0),
                              writes=[b_z128])
                        sc.op("vector", lambda e: e.memset(sb_srun[:], 0.0),
                              writes=list(b_srun), fastout=True)
                    sc.op("gpsimd",
                          lambda e: e.dma_start(out=sb_cs3[0:1, 0:INTER],
                                                in_=d_drep[:]),
                          writes=[b_cs3], dma=True)
                    for q in range(16):
                        sc.op("tensor",
                              lambda e, q=q: e.matmul(
                                  banks[q % 2][:, 0:512],
                                  sb_ones3[0:1, :],
                                  sb_cs3[0:1, q * 512:(q + 1) * 512],
                                  start=True, stop=True),
                              reads=[b_cs3, b_consts], writes=[b_bank[q % 2]])
                        sc.op("vector",
                              lambda e, q=q: e.tensor_copy(
                                  sb_dfree[:, q * 512:(q + 1) * 512],
                                  banks[q % 2][:, 0:512]),
                              reads=[b_bank[q % 2]], writes=[b_dfree],
                              fastout=True)
                    sc.op("gpsimd",
                          lambda e: e.dma_start(out=sb_cs3[0:1, 0:INTER],
                                                in_=d_nrep[:]),
                          reads=[b_cs3], writes=[b_cs3], dma=True)
                    for q in range(16):
                        sc.op("tensor",
                              lambda e, q=q: e.matmul(
                                  banks[q % 2][:, 0:512],
                                  sb_ones3[0:1, :],
                                  sb_cs3[0:1, q * 512:(q + 1) * 512],
                                  start=True, stop=True),
                              reads=[b_cs3, b_consts], writes=[b_bank[q % 2]])
                        sc.op("vector",
                              lambda e, q=q: e.tensor_copy(
                                  sb_nrep[:, q * 512:(q + 1) * 512],
                                  banks[q % 2][:, 0:512]),
                              reads=[b_bank[q % 2]], writes=[b_nrep],
                              fastout=True)
                    t_y = tap("ytm", [NCH, 128, INTER], BF16)
                    t_yn = tap("yntm", [NCH, 128, INTER], BF16)
                    for c in range(NCH):
                        emit_prep(c, full=True)
                        sc.op("vector",
                              lambda e: e.tensor_copy(sb_srun_bf[:], sb_srun[:]),
                              reads=list(b_srun), writes=[b_srunbf],
                              fastout=True)
                        for j in range(16):
                            emit_states_batch(c, j, full=True)
                        if t_y is not None:
                            tk = sc.op("sync",
                                       lambda e, c=c, t_y=t_y: e.dma_start(
                                           out=t_y[c], in_=sb_ytm[:]),
                                       reads=[b_ytm], dma=True)
                            out_toks.append(tk)
                        sc.op("vector",
                              lambda e: e.tensor_tensor(sb_xtm[:], sb_ytm[:],
                                                        sb_gtm[:], ALU.mult),
                              reads=[b_ytm, b_gtm], writes=[b_xtm],
                              fastout=True)
                        for g in range(NG):
                            sc.op("scalar",
                                  lambda e, g=g: e.activation(
                                      sb_scores[:, 0:1024],
                                      sb_xtm[:, g * 1024:(g + 1) * 1024],
                                      AF.Square, accum_out=sb_rms[:, g:g + 1]),
                                  reads=[b_xtm], writes=[b_scores, b_rms],
                                  sync_after=True)
                        sc.op("vector",
                              lambda e: e.tensor_scalar(
                                  sb_rms[:, 8:16], sb_rms[:, 0:8], 1.0 / 1024.0,
                                  EPS_RMS, ALU.mult, ALU.add),
                              reads=[b_rms], writes=[b_rms])
                        sc.op("scalar",
                              lambda e: e.activation(sb_rms[:, 8:16],
                                                     sb_rms[:, 8:16], AF.Ln),
                              reads=[b_rms], writes=[b_rms])
                        sc.op("scalar",
                              lambda e: e.activation(sb_rms[:, 8:16],
                                                     sb_rms[:, 8:16], AF.Exp,
                                                     scale=-0.5),
                              reads=[b_rms], writes=[b_rms])
                        for g in range(NG):
                            sc.op("vector",
                                  lambda e, g=g: e.tensor_scalar(
                                      sb_gtm[:, g * 1024:(g + 1) * 1024],
                                      sb_xtm[:, g * 1024:(g + 1) * 1024],
                                      sb_rms[:, 8 + g:9 + g], None, ALU.mult),
                                  reads=[b_xtm, b_rms], writes=[b_gtm],
                                  fastout=True)
                        sc.op("vector",
                              lambda e: e.tensor_tensor(sb_ytm[:], sb_gtm[:],
                                                        sb_nrep[:], ALU.mult),
                              reads=[b_gtm, b_nrep], writes=[b_ytm],
                              fastout=True)
                        sc.op("sync",
                              lambda e, c=c: e.dma_start(out=d_YNTM[c],
                                                         in_=sb_ytm[:]),
                              reads=[b_ytm], writes=[b_YNTM[c]], dma=True)
                        if t_yn is not None:
                            tk = sc.op("sync",
                                       lambda e, c=c, t_yn=t_yn: e.dma_start(
                                           out=t_yn[c], in_=sb_ytm[:]),
                                       reads=[b_ytm], dma=True)
                            out_toks.append(tk)

        if "ph4" in phases:
            sc.fence()
            with contextlib.ExitStack() as st4:
                e4 = st4.enter_context
                sb_yn = e4(nc.sbuf_tensor("yn", [128, 64 * 1024], BF16))
                sb_wo = [e4(nc.sbuf_tensor(f"wo{i}", [128, 64 * 128], BF16))
                         for i in range(2)]
                sb_so = [e4(nc.sbuf_tensor(f"so{i}", [128, 1024], BF16))
                        for i in range(2)]
                b_yn = Buf("yn")
                b_wo = [Buf(f"wo{i}") for i in range(2)]
                b_so = [Buf(f"so{i}") for i in range(2)]
                t_sfm = tap("sfm", [32, 128, TLOC], BF16)

                for half in range(2):
                    for kc in range(64):
                        src = d_YNTM[8 * half:8 * half + 8, :,
                                     kc * 128:(kc + 1) * 128].rearrange(
                                         "c t f -> (c t) f")
                        sc.op("sync",
                              lambda e, src=src, kc=kc: e.dma_start(
                                  out=sb_yn[:, kc * 1024:(kc + 1) * 1024],
                                  in_=src, transpose=True),
                              reads=b_YNTM, writes=[b_yn], dma=True)
                    for mc in range(32):
                        wb, bw = sb_wo[mc % 2], b_wo[mc % 2]
                        sc.op("gpsimd",
                              lambda e, mc=mc, wb=wb: e.dma_start(
                                  out=wb[:],
                                  in_=d_wout[mc].rearrange("k kc m -> k (kc m)")),
                              writes=[bw], dma=True)
                        for kc in range(64):
                            for ci in range(2):
                                sc.op("tensor",
                                      lambda e, kc=kc, ci=ci, wb=wb: e.matmul(
                                          banks[ci][:, 0:512],
                                          wb[:, kc * 128:(kc + 1) * 128],
                                          sb_yn[:, kc * 1024 + ci * 512:
                                                kc * 1024 + ci * 512 + 512],
                                          start=(kc == 0), stop=(kc == 63)),
                                      reads=[bw, b_yn], writes=[b_bank[ci]])
                        so, bso = sb_so[mc % 2], b_so[mc % 2]
                        for ci in range(2):
                            eng = "vector" if ci == 0 else "scalar"
                            def evo(e, ci=ci, so=so, eng=eng):
                                if eng == "vector":
                                    return e.tensor_copy(
                                        so[:, ci * 512:(ci + 1) * 512],
                                        banks[ci][:, 0:512])
                                return e.copy(so[:, ci * 512:(ci + 1) * 512],
                                              banks[ci][:, 0:512])
                            sc.op(eng, evo, reads=[b_bank[ci]], writes=[bso])
                        sc.op("sync",
                              lambda e, mc=mc, half=half, so=so: e.dma_start(
                                  out=d_SFM[mc][:, half * 1024:(half + 1) * 1024],
                                  in_=so[:]),
                              reads=[bso], writes=[b_SFM[mc]], dma=True)
                        if t_sfm is not None:
                            tk = sc.op("sync",
                                       lambda e, mc=mc, half=half, so=so:
                                       e.dma_start(
                                           out=t_sfm[mc, :,
                                                     half * 1024:(half + 1) * 1024],
                                           in_=so[:]),
                                       reads=[bso], dma=True)
                            out_toks.append(tk)

        if "ph5" in phases:
            sc.fence()
            with contextlib.ExitStack() as st5:
                e5 = st5.enter_context
                sb_xn5 = e5(nc.sbuf_tensor("xn5", [128, 32 * 1024], BF16))
                sb_wb5 = [e5(nc.sbuf_tensor(f"wb5{i}", [128, 4096], BF16))
                          for i in range(2)]
                sb_A5 = e5(nc.sbuf_tensor("A5", [128, 4096], BF16))
                sb_B5 = e5(nc.sbuf_tensor("B5", [128, 4096], BF16))
                sb_wr5 = e5(nc.sbuf_tensor("wr5", [128, 256], BF16))
                sb_lg = e5(nc.sbuf_tensor("lg", [8, 1024], F32))
                sb_tp = e5(nc.sbuf_tensor("tp", [128, 48], F32))
                sb_wtm = e5(nc.sbuf_tensor("wtm", [128, 8], F32))
                sb_wfm = e5(nc.sbuf_tensor("wfm", [8, 1024], F32))
                sb_w8 = e5(nc.sbuf_tensor("w8", [128, 1024], F32))
                sb_mid = e5(nc.sbuf_tensor("mid", [128, 1024], BF16))
                sb_s5 = e5(nc.sbuf_tensor("s5", [128, 1024], BF16))
                sb_x5 = e5(nc.sbuf_tensor("x5", [128, 1024], F32))
                sb_g5 = e5(nc.sbuf_tensor("g5", [128, 1024], BF16))
                sb_o5 = e5(nc.sbuf_tensor("o5", [128, 1024], F32))
                b_xn5 = Buf("xn5")
                b_wb5 = [Buf(f"wb5{i}") for i in range(2)]
                b_A5, b_B5, b_wr5 = Buf("A5"), Buf("B5"), Buf("wr5")
                b_lg, b_tp, b_wtm, b_wfm = Buf("lg"), Buf("tp"), Buf("wtm"), Buf("wfm")
                b_w8, b_mid = Buf("w8"), Buf("mid")
                b_s5, b_x5, b_g5, b_o5 = Buf("s5"), Buf("x5"), Buf("g5"), Buf("o5")

                sc.op("gpsimd",
                      lambda e: e.dma_start(
                          out=sb_A5[:],
                          in_=d_loraA.rearrange("k kc j -> k (kc j)")),
                      writes=[b_A5], dma=True)
                sc.op("gpsimd",
                      lambda e: e.dma_start(
                          out=sb_B5[:],
                          in_=d_loraB.rearrange("j mc m -> j (mc m)")),
                      writes=[b_B5], dma=True)
                sc.op("gpsimd",
                      lambda e: e.dma_start(out=sb_wr5[:], in_=d_wrout[:]),
                      writes=[b_wr5], dma=True)

                t_w = tap("wdense", [2, 8, 1024], F32)
                t_ffn = tap("gsilu", [32, 128, TLOC], BF16)

                for half in range(2):
                    src = d_XNFM[:].rearrange("p (kc t) -> p kc t", kc=NKC)[
                        :, :, LOOK + half * 1024: LOOK + (half + 1) * 1024]
                    sc.op("sync",
                          lambda e, src=src: e.dma_start(
                              out=sb_xn5[:].rearrange("p (kc t) -> p kc t",
                                                      kc=NKC),
                              in_=src),
                          reads=[b_XNFMd], writes=[b_xn5], dma=True)
                    for kc in range(NKC):
                        for ci in range(2):
                            sc.op("tensor",
                                  lambda e, kc=kc, ci=ci: e.matmul(
                                      banks[6 + ci][0:8, 0:512],
                                      sb_wr5[:, kc * 8:(kc + 1) * 8],
                                      sb_xn5[:, kc * 1024 + ci * 512:
                                             kc * 1024 + ci * 512 + 512],
                                      start=(kc == 0), stop=(kc == NKC - 1)),
                                  reads=[b_wr5, b_xn5], writes=[b_bank[6 + ci]])
                    for ci in range(2):
                        sc.op("vector",
                              lambda e, ci=ci: e.tensor_copy(
                                  sb_lg[:, ci * 512:(ci + 1) * 512],
                                  banks[6 + ci][0:8, 0:512]),
                              reads=[b_bank[6 + ci]], writes=[b_lg])
                    for tc in range(8):
                        lgt = sb_lg[:, tc * 128:(tc + 1) * 128]
                        sc.op("tensor",
                              lambda e, lgt=lgt: e.transpose(
                                  banks[2][0:128, 0:8], lgt,
                                  sb_ident_f[0:8, 0:8]),
                              reads=[b_lg, b_consts], writes=[b_bank[2]])
                        sc.op("vector",
                              lambda e: e.tensor_copy(sb_tp[:, 0:8],
                                                      banks[2][0:128, 0:8]),
                              reads=[b_bank[2]], writes=[b_tp])
                        sc.op("vector",
                              lambda e: e.reduce_max(sb_tp[:, 8:9],
                                                     sb_tp[:, 0:8], axis=AX.X),
                              reads=[b_tp], writes=[b_tp], sync_after=True)
                        sc.op("vector",
                              lambda e: e.tensor_scalar(
                                  sb_tp[:, 16:24], sb_tp[:, 0:8],
                                  sb_tp[:, 8:9], None, ALU.subtract),
                              reads=[b_tp], writes=[b_tp])
                        sc.op("vector",
                              lambda e: e.tensor_scalar(
                                  sb_tp[:, 24:32], sb_tp[:, 16:24], 0.0, -1.0e30,
                                  ALU.is_ge, ALU.mult),
                              reads=[b_tp], writes=[b_tp])
                        sc.op("vector",
                              lambda e: e.tensor_tensor(
                                  sb_tp[:, 24:32], sb_tp[:, 16:24],
                                  sb_tp[:, 24:32], ALU.add),
                              reads=[b_tp], writes=[b_tp])
                        sc.op("vector",
                              lambda e: e.reduce_max(sb_tp[:, 9:10],
                                                     sb_tp[:, 24:32], axis=AX.X),
                              reads=[b_tp], writes=[b_tp], sync_after=True)
                        sc.op("scalar",
                              lambda e: e.activation(sb_tp[:, 32:40],
                                                     sb_tp[:, 16:24], AF.Exp),
                              reads=[b_tp], writes=[b_tp])
                        sc.op("vector",
                              lambda e: e.tensor_scalar(
                                  sb_tp[:, 24:32], sb_tp[:, 16:24],
                                  sb_tp[:, 9:10], None, ALU.is_ge),
                              reads=[b_tp], writes=[b_tp])
                        sc.op("scalar",
                              lambda e: e.activation(sb_tp[:, 10:11],
                                                     sb_tp[:, 9:10], AF.Exp),
                              reads=[b_tp], writes=[b_tp])
                        sc.op("vector",
                              lambda e: e.tensor_scalar(
                                  sb_tp[:, 10:11], sb_tp[:, 10:11], 1.0, None,
                                  ALU.add),
                              reads=[b_tp], writes=[b_tp])
                        sc.op("vector",
                              lambda e: e.reciprocal(sb_tp[:, 11:12],
                                                     sb_tp[:, 10:11]),
                              reads=[b_tp], writes=[b_tp])
                        sc.op("vector",
                              lambda e: e.tensor_scalar(
                                  sb_tp[:, 11:12], sb_tp[:, 11:12],
                                  LORA_SCALE, None, ALU.mult),
                              reads=[b_tp], writes=[b_tp])
                        sc.op("vector",
                              lambda e: e.scalar_tensor_tensor(
                                  sb_wtm[:, 0:8], sb_tp[:, 32:40],
                                  sb_tp[:, 11:12], sb_tp[:, 24:32],
                                  ALU.mult, ALU.mult),
                              reads=[b_tp], writes=[b_wtm])
                        sc.op("tensor",
                              lambda e: e.transpose(banks[2][0:8, 0:128],
                                                    sb_wtm[:, 0:8],
                                                    sb_ident_f[:]),
                              reads=[b_wtm, b_consts], writes=[b_bank[2]])
                        sc.op("vector",
                              lambda e, tc=tc: e.tensor_copy(
                                  sb_wfm[:, tc * 128:(tc + 1) * 128],
                                  banks[2][0:8, 0:128]),
                              reads=[b_bank[2]], writes=[b_wfm])
                    if t_w is not None:
                        tk = sc.op("sync",
                                   lambda e, half=half, t_w=t_w: e.dma_start(
                                       out=t_w[half], in_=sb_wfm[:]),
                                   reads=[b_wfm], dma=True)
                        out_toks.append(tk)
                    for kc in range(NKC):
                        for ci in range(2):
                            sc.op("tensor",
                                  lambda e, kc=kc, ci=ci: e.matmul(
                                      banks[4 + ci][:, 0:512],
                                      sb_A5[:, kc * 128:(kc + 1) * 128],
                                      sb_xn5[:, kc * 1024 + ci * 512:
                                             kc * 1024 + ci * 512 + 512],
                                      start=(kc == 0), stop=(kc == NKC - 1)),
                                  reads=[b_A5, b_xn5], writes=[b_bank[4 + ci]])
                    for ci in range(2):
                        sc.op("tensor",
                              lambda e, ci=ci: e.matmul(
                                  banks[2 + ci][:, 0:512], sb_eexp[:],
                                  sb_wfm[0:8, ci * 512:(ci + 1) * 512],
                                  start=True, stop=True),
                              reads=[b_wfm, b_consts], writes=[b_bank[2 + ci]])
                        sc.op("vector",
                              lambda e, ci=ci: e.tensor_copy(
                                  sb_w8[:, ci * 512:(ci + 1) * 512],
                                  banks[2 + ci][:, 0:512]),
                              reads=[b_bank[2 + ci]], writes=[b_w8])
                        sc.op("vector",
                              lambda e, ci=ci: e.tensor_tensor(
                                  sb_mid[:, ci * 512:(ci + 1) * 512],
                                  banks[4 + ci][:, 0:512],
                                  sb_w8[:, ci * 512:(ci + 1) * 512], ALU.mult),
                              reads=[b_bank[4 + ci], b_w8], writes=[b_mid])
                    for mc in range(32):
                        wb, bw = sb_wb5[mc % 2], b_wb5[mc % 2]
                        sc.op("gpsimd",
                              lambda e, mc=mc, wb=wb: e.dma_start(
                                  out=wb[:],
                                  in_=d_wbase[mc].rearrange("k kc m -> k (kc m)")),
                              writes=[bw], dma=True)
                        sc.op("sync",
                              lambda e, mc=mc, half=half: e.dma_start(
                                  out=sb_s5[:],
                                  in_=d_SFM[mc][:, half * 1024:(half + 1) * 1024]),
                              reads=[b_SFM[mc]], writes=[b_s5], dma=True)
                        sc.op("sync",
                              lambda e, mc=mc, half=half: e.dma_start(
                                  out=sb_x5[:],
                                  in_=d_xfm[mc][:, half * 1024:(half + 1) * 1024]),
                              writes=[b_x5], dma=True)
                        for ci in range(2):
                            for kc in range(NKC):
                                sc.op("tensor",
                                      lambda e, kc=kc, ci=ci, wb=wb: e.matmul(
                                          banks[ci][:, 0:512],
                                          wb[:, kc * 128:(kc + 1) * 128],
                                          sb_xn5[:, kc * 1024 + ci * 512:
                                                 kc * 1024 + ci * 512 + 512],
                                          start=(kc == 0), stop=False),
                                      reads=[bw, b_xn5], writes=[b_bank[ci]])
                            sc.op("tensor",
                                  lambda e, mc=mc, ci=ci: e.matmul(
                                      banks[ci][:, 0:512],
                                      sb_B5[:, mc * 128:(mc + 1) * 128],
                                      sb_mid[:, ci * 512:(ci + 1) * 512],
                                      start=False, stop=True),
                                  reads=[b_B5, b_mid], writes=[b_bank[ci]])
                            sc.op("scalar",
                                  lambda e, ci=ci: e.activation(
                                      sb_g5[:, ci * 512:(ci + 1) * 512],
                                      banks[ci][:, 0:512], AF.Silu),
                                  reads=[b_bank[ci]], writes=[b_g5])
                        sc.op("vector",
                              lambda e: e.tensor_tensor(
                                  sb_o5[:], sb_s5[:], sb_g5[:], ALU.mult),
                              reads=[b_s5, b_g5], writes=[b_o5])
                        sc.op("vector",
                              lambda e: e.tensor_tensor(
                                  sb_o5[:], sb_x5[:], sb_o5[:], ALU.add),
                              reads=[b_x5, b_o5], writes=[b_o5])
                        tk = sc.op("sync",
                                   lambda e, mc=mc, half=half: e.dma_start(
                                       out=d_out[mc][:, half * 1024:
                                                     (half + 1) * 1024],
                                       in_=sb_o5[:]),
                                   reads=[b_o5], dma=True)
                        out_toks.append(tk)
                        if t_ffn is not None:
                            tk = sc.op("sync",
                                       lambda e, mc=mc, half=half: e.dma_start(
                                           out=t_ffn[mc, :, half * 1024:
                                                     (half + 1) * 1024],
                                           in_=sb_g5[:]),
                                       reads=[b_g5], dma=True)
                            out_toks.append(tk)

        sc.op("sync", lambda e: e.nop(), deps=out_toks + sc.barrier_toks())

        block = ent(nc.Block())
        sc.emit(block, sems)

    return nc, tap_handles



def _prep_shared(inputs):
    Win = np.asarray(inputs["L0_attn_in_proj_w"])
    Wout = np.asarray(inputs["L0_attn_out_proj_w"])
    Wb = np.asarray(inputs["L0_ffn_base_W"])
    Wr = np.asarray(inputs["L0_ffn_router_W"])
    lA = np.asarray(inputs["L0_ffn_lora_A"])
    lB = np.asarray(inputs["L0_ffn_lora_B"])
    convw = np.asarray(inputs["L0_attn_conv_w"])
    convb = np.asarray(inputs["L0_attn_conv_b"])
    dtb = np.asarray(inputs["L0_attn_dt_bias"])
    Dv = np.asarray(inputs["L0_attn_D"])
    normw = np.asarray(inputs["L0_attn_norm_w"])

    perm = np.concatenate([
        np.arange(INTER, INTER + INTER),
        np.arange(INTER + INTER, INTER + INTER + 1024),
        np.arange(INTER + INTER + 1024, INTER + CONV_DIM),
        np.arange(INTER + CONV_DIM, PROJ),
        np.arange(0, INTER),
    ])
    Winp = Win[perm]
    w_in = np.ascontiguousarray(
        Winp.reshape(NSLAB, 128, NKC, 128).transpose(0, 3, 2, 1))

    w_out = np.ascontiguousarray(
        Wout.reshape(32, 128, 64, 128).transpose(0, 3, 2, 1))
    w_base = np.ascontiguousarray(
        Wb.reshape(32, 128, 32, 128).transpose(0, 3, 2, 1))
    w_router = np.ascontiguousarray(
        Wr.reshape(8, 32, 128).transpose(2, 1, 0).reshape(128, 256))
    loraA = np.ascontiguousarray(
        lA.transpose(1, 0, 2).reshape(NKC, 128, 8 * 16).transpose(1, 0, 2))
    loraB = np.ascontiguousarray(
        lB.reshape(128, 32, 128))

    cw = convw[:, 0, :].reshape(80, 128, 4).transpose(1, 0, 2).reshape(128, 320)
    cb = convb.reshape(80, 128).T
    a_col = (-np.arange(1, NH + 1, dtype=np.float32)).reshape(128, 1)
    d_rep = np.repeat(Dv.astype(np.float32), HD).reshape(1, INTER)
    normw_g = np.ascontiguousarray(normw.reshape(64, 128).T)

    ident_b = np.eye(128, dtype=np.float32).astype(ml_dtypes.bfloat16)
    ident_f = np.eye(128, dtype=np.float32)
    trimask = (np.arange(128)[:, None] <= np.arange(128)[None, :]).astype(np.float32)
    e_expand = (np.arange(128)[None, :] // 16 == np.arange(8)[:, None]).astype(
        np.float32)
    ones3 = np.ones((3, 128), dtype=np.float32).astype(ml_dtypes.bfloat16)

    return dict(w_in=w_in, w_out=w_out, w_base=w_base, w_router=w_router,
                loraA=loraA, loraB=loraB, convw=np.ascontiguousarray(cw),
                convb=np.ascontiguousarray(cb),
                dt_bias=dtb.reshape(128, 1).astype(np.float32),
                a_col=a_col, d_rep=d_rep, normw_g=normw_g,
                normw_rep=normw.astype(np.float32).reshape(1, INTER),
                ident_b=ident_b, ident_f=ident_f, trimask=trimask,
                e_expand=e_expand, ones3=ones3)


def make_in_maps(inputs):
    x = np.asarray(inputs["L0_attn_hidden_states"])
    shared = _prep_shared(inputs)
    nonce = int(np.random.randint(1, 2**30))
    maps = []
    for c in range(N_CORES):
        b, half = c // 2, c % 2
        t0 = half * TLOC
        xln = np.empty((TEXT, DM), dtype=np.float32)
        if half == 0:
            xln[0:LOOK] = 0.0
        else:
            xln[0:LOOK] = x[b, t0 - LOOK:t0]
        xln[LOOK:] = x[b, t0:t0 + TLOC]
        xfm = np.ascontiguousarray(
            x[b, t0:t0 + TLOC].T.reshape(NKC, 128, TLOC))
        par = np.array([[half, nonce, 0, 0]], dtype=np.int32)
        m = dict(shared)
        m.update(xln=xln, xfm=xfm, par=par)
        maps.append(m)
    return maps


def assemble(results):
    out = np.empty((B, S, DM), dtype=np.float32)
    for c in range(N_CORES):
        b, half = c // 2, c % 2
        t0 = half * TLOC
        o = np.asarray(results[c]["out"])
        out[b, t0:t0 + TLOC, :] = o.reshape(DM, TLOC).T
    return out


def kernel(**inputs):
    maps = make_in_maps(inputs)
    nc, _ = build_program()
    res = run_bass_kernel_spmd(nc, maps, core_ids=list(range(N_CORES)))
    return assemble(res.results)


# version 02
# speedup vs baseline: 1.0143x; 1.0143x over previous
import contextlib

import numpy as np
import ml_dtypes
import concourse.bass as bass
import concourse.mybir as mybir
from concourse.bass_utils import run_bass_kernel_spmd

F32 = mybir.dt.float32
BF16 = mybir.dt.bfloat16
I32 = mybir.dt.int32
AF = mybir.ActivationFunctionType
ALU = mybir.AluOpType
AX = mybir.AxisListType

B, S, DM = 4, 4096, 4096
NH, HD, SSZ, NG, KCONV = 128, 64, 128, 8, 4
INTER = NH * HD
CONV_DIM = INTER + 2 * NG * SSZ
PROJ = INTER + CONV_DIM + NH
CHUNK = 128
TLOC = 2048
LOOK = 3
TEXT = TLOC + LOOK
NCH = TLOC // CHUNK
NKC = DM // 128
NSLAB = PROJ // 128
SLAB_X0, SLAB_B0, SLAB_C0, SLAB_DT, SLAB_G0 = 0, 64, 72, 80, 81
DT_MIN, DT_MAX, EPS_RMS, EPS_LN = 1e-3, 100.0, 1e-5, 1e-5
LORA_SCALE = 2.0

N_CORES = 8

PHASES = ("ph1", "ph2", "ph3a", "hand", "ph3", "ph4", "ph5")
TAPS = ()


ENGS = ("tensor", "vector", "scalar", "gpsimd", "sync")


class Tok:
    __slots__ = ("eng", "idx", "asyncc")

    def __init__(self, eng, idx, asyncc):
        self.eng, self.idx, self.asyncc = eng, idx, asyncc


class Buf:
    __slots__ = ("w", "r", "name")

    def __init__(self, name=""):
        self.w, self.r, self.name = None, [], name


class Sched:
    def __init__(self):
        self.ops = []
        self.nidx = {e: 0 for e in ENGS}
        self.last = {}
        self.fence_toks = []

    def fence(self):
        self.fence_toks = self.barrier_toks()

    def op(self, eng, fn, reads=(), writes=(), deps=(), dma=False,
           sync_after=False, fastout=False):
        d = list(self.fence_toks)
        for b in reads:
            if b.w is not None:
                d.append(b.w)
        for b in writes:
            if b.w is not None:
                d.append(b.w)
            d.extend(b.r)
        d.extend(deps)
        tok = Tok(eng, self.nidx[eng], dma or sync_after)
        self.nidx[eng] += 1
        self.ops.append(dict(eng=eng, fn=fn, deps=d, tok=tok,
                             inc=16 if dma else 1, needed=False,
                             fastout=False and fastout and not (dma or sync_after)))
        for b in reads:
            b.r.append(tok)
        for b in writes:
            b.w = tok
            b.r = []
        self.last[eng] = tok
        return tok

    def barrier_toks(self):
        return [t for t in (self.last.get(e) for e in ENGS) if t is not None]

    def emit(self, block, sems):
        per_eng = {e: [] for e in ENGS}
        for o in self.ops:
            per_eng[o["eng"]].append(o)
        def edge_needed(dep_tok, consumer_eng):
            if dep_tok.eng != consumer_eng:
                return True
            prod = per_eng[dep_tok.eng][dep_tok.idx]
            if dep_tok.eng == "tensor":
                return dep_tok.asyncc
            return dep_tok.asyncc or not prod["fastout"]

        for o in self.ops:
            if o["tok"].asyncc:
                o["needed"] = True
            for t in o["deps"]:
                if edge_needed(t, o["eng"]):
                    per_eng[t.eng][t.idx]["needed"] = True
        cum = {}
        for e in ENGS:
            c = 0
            for o in per_eng[e]:
                if o["needed"]:
                    c += o["inc"]
                cum[(e, o["tok"].idx)] = c
        def mk_section(ename):
            def go(eng):
                hw = {e: 0 for e in ENGS}
                for o in per_eng[ename]:
                    for t in o["deps"]:
                        if not edge_needed(t, ename):
                            continue
                        v = cum[(t.eng, t.idx)]
                        if v > hw[t.eng]:
                            eng.wait_ge(sems[t.eng], v)
                            hw[t.eng] = v
                    inst = o["fn"](eng)
                    if o["needed"]:
                        if inst is None:
                            inst = eng.nop()
                        if o["inc"] == 16 or ename == "tensor":
                            inst.then_inc(sems[ename], o["inc"])
                        else:
                            eng.drain(fusable=True).then_inc(
                                sems[ename], o["inc"])
            return go

        for e in ENGS:
            if per_eng[e]:
                getattr(block, e)(mk_section(e))



def build_program(phases=PHASES, taps=TAPS):
    nc = bass.Bass("TRN2", target_bir_lowering=False)
    sc = Sched()
    out_toks = []

    d_xln = nc.dram_tensor("xln", [TEXT, DM], F32, kind="ExternalInput").ap()
    d_xfm = nc.dram_tensor("xfm", [NKC, 128, TLOC], F32, kind="ExternalInput").ap()
    d_par = nc.dram_tensor("par", [1, 4], I32, kind="ExternalInput").ap()
    d_win = nc.dram_tensor("w_in", [NSLAB, 128, NKC, 128], F32, kind="ExternalInput").ap()
    d_wout = nc.dram_tensor("w_out", [32, 128, 64, 128], F32, kind="ExternalInput").ap()
    d_wbase = nc.dram_tensor("w_base", [32, 128, 32, 128], F32, kind="ExternalInput").ap()
    d_wrout = nc.dram_tensor("w_router", [128, 256], F32, kind="ExternalInput").ap()
    d_loraA = nc.dram_tensor("loraA", [128, NKC, 128], F32, kind="ExternalInput").ap()
    d_loraB = nc.dram_tensor("loraB", [128, 32, 128], F32, kind="ExternalInput").ap()
    d_convw = nc.dram_tensor("convw", [128, 320], F32, kind="ExternalInput").ap()
    d_convb = nc.dram_tensor("convb", [128, 80], F32, kind="ExternalInput").ap()
    d_dtb = nc.dram_tensor("dt_bias", [128, 1], F32, kind="ExternalInput").ap()
    d_acol = nc.dram_tensor("a_col", [128, 1], F32, kind="ExternalInput").ap()
    d_drep = nc.dram_tensor("d_rep", [1, INTER], F32, kind="ExternalInput").ap()
    d_normg = nc.dram_tensor("normw_g", [128, 64], F32, kind="ExternalInput").ap()
    d_nrep = nc.dram_tensor("normw_rep", [1, INTER], F32, kind="ExternalInput").ap()
    d_identb = nc.dram_tensor("ident_b", [128, 128], BF16, kind="ExternalInput").ap()
    d_identf = nc.dram_tensor("ident_f", [128, 128], F32, kind="ExternalInput").ap()
    d_trimask = nc.dram_tensor("trimask", [128, 128], F32, kind="ExternalInput").ap()
    d_eexp = nc.dram_tensor("e_expand", [8, 128], F32, kind="ExternalInput").ap()
    d_ones3 = nc.dram_tensor("ones3", [3, 128], BF16, kind="ExternalInput").ap()

    d_out = nc.dram_tensor("out", [32, 128, TLOC], F32, kind="ExternalOutput").ap()

    d_XNFM = nc.dram_tensor("XNFM", [128, NKC * TEXT], BF16, kind="Internal").ap()
    d_XFMS = nc.dram_tensor("XFMS", [64, 128, TLOC], BF16, kind="Internal").ap()
    d_GFMS = nc.dram_tensor("GFMS", [64, 128, TLOC], BF16, kind="Internal").ap()
    d_BCFM = nc.dram_tensor("BCFM", [16, 128, TLOC], BF16, kind="Internal").ap()
    d_YNTM = nc.dram_tensor("YNTM", [NCH, 128, INTER], BF16, kind="Internal").ap()
    d_SFM = nc.dram_tensor("SFM", [32, 128, TLOC], BF16, kind="Internal").ap()
    d_CSR = nc.dram_tensor("CSR", [3, CHUNK * 128], BF16, kind="Internal").ap()
    d_SHST = nc.dram_tensor("SH_STATE", [128, INTER], F32, kind="Internal",
                            addr_space="Shared").ap()
    d_SHFL = nc.dram_tensor("SH_FLAG", [1, 1], I32, kind="Internal",
                            addr_space="Shared").ap()

    tap_handles = {}
    def tap(name, shape, dtype):
        if name in taps:
            tap_handles[name] = nc.dram_tensor("tap_" + name, shape, dtype,
                                               kind="ExternalOutput").ap()
            return tap_handles[name]
        return None

    with contextlib.ExitStack() as st:
        ent = st.enter_context
        sems = {e: ent(nc.semaphore("S_" + e)) for e in ENGS}

        banks = [ent(nc.psum_tensor(f"bank{i}", [128, 512], F32)) for i in range(8)]

        sb_ident_b = ent(nc.sbuf_tensor("c_identb", [128, 128], BF16))
        sb_ident_f = ent(nc.sbuf_tensor("c_identf", [128, 128], F32))
        sb_trimask = ent(nc.sbuf_tensor("c_trimask", [128, 128], F32))
        sb_convw = ent(nc.sbuf_tensor("c_convw", [128, 320], F32))
        sb_convb = ent(nc.sbuf_tensor("c_convb", [128, 80], F32))
        sb_dtb = ent(nc.sbuf_tensor("c_dtb", [128, 1], F32))
        sb_acol = ent(nc.sbuf_tensor("c_acol", [128, 1], F32))
        sb_normg = ent(nc.sbuf_tensor("c_normg", [128, 64], F32))
        sb_eexp = ent(nc.sbuf_tensor("c_eexp", [8, 128], F32))
        sb_ones3 = ent(nc.sbuf_tensor("c_ones3", [3, 128], BF16))
        sb_par = ent(nc.sbuf_tensor("c_par", [1, 4], I32))

        b_consts = Buf("consts")
        b_bank = [Buf(f"bank{i}") for i in range(8)]

        sb_dtclip = ent(nc.sbuf_tensor("dtclip", [128, TLOC], F32))
        b_dtclip = Buf("dtclip")

        b_XFMS = [Buf(f"XFMS{i}") for i in range(64)]
        b_GFMS = [Buf(f"GFMS{i}") for i in range(64)]
        b_BCFM = [Buf(f"BCFM{i}") for i in range(16)]
        b_XNFMd = Buf("XNFMd")
        b_YNTM = [Buf(f"YNTM{i}") for i in range(NCH)]
        b_SFM = [Buf(f"SFM{i}") for i in range(32)]

        const_pairs = [
            (sb_ident_b, d_identb), (sb_ident_f, d_identf),
            (sb_trimask, d_trimask), (sb_convw, d_convw), (sb_convb, d_convb),
            (sb_dtb, d_dtb), (sb_acol, d_acol), (sb_normg, d_normg),
            (sb_eexp, d_eexp), (sb_ones3, d_ones3), (sb_par, d_par),
        ]
        for sb_c, d_c in const_pairs:
            sc.op("sync",
                  lambda e, sb_c=sb_c, d_c=d_c: e.dma_start(out=sb_c[:], in_=d_c[:]),
                  writes=[b_consts], dma=True)

        with contextlib.ExitStack() as st12:
            e12 = st12.enter_context
            sb_xnfm = e12(nc.sbuf_tensor("xnfm", [128, NKC * TEXT], BF16))
            b_xnfm = Buf("xnfm")
            b_xnfm_cols = [Buf(f"xnfmc{k}") for k in range(NKC)]

            if "ph1" in phases:
                with contextlib.ExitStack() as st1:
                    e1 = st1.enter_context
                    sb_xld = [e1(nc.sbuf_tensor(f"xld{i}", [128, DM], BF16)) for i in range(2)]
                    sb_xnt = [e1(nc.sbuf_tensor(f"xnt{i}", [128, DM], BF16)) for i in range(2)]
                    sb_sq = e1(nc.sbuf_tensor("sq", [128, 8], F32))
                    sb_sqd = e1(nc.sbuf_tensor("sqd", [128, DM], F32))
                    b_xld = [Buf(f"xld{i}") for i in range(2)]
                    b_xnt = [Buf(f"xnt{i}") for i in range(2)]
                    b_sq = Buf("sq")
                    b_sqd = Buf("sqd")
                    b_tpb = b_bank

                    ntile = (TEXT + 127) // 128
                    for i in range(ntile):
                        p = min(128, TEXT - i * 128)
                        bi = i % 2
                        xl, bx = sb_xld[bi], b_xld[bi]
                        xo, bo = sb_xnt[bi], b_xnt[bi]
                        sc.op("gpsimd",
                              lambda e, i=i, p=p, xl=xl: e.dma_start(
                                  out=xl[0:p, :], in_=d_xln[i * 128:i * 128 + p, :]),
                              writes=[bx], dma=True)
                        sc.op("scalar",
                              lambda e, p=p, xl=xl: e.activation(
                                  sb_sqd[0:p, :], xl[0:p, :], AF.Square,
                                  accum_out=sb_sq[0:p, 0:1]),
                              reads=[bx], writes=[b_sqd, b_sq], sync_after=True)
                        sc.op("vector",
                              lambda e, p=p, xl=xl: e.reduce_sum(
                                  sb_sq[0:p, 1:2], xl[0:p, :], axis=AX.X),
                              reads=[bx], writes=[b_sq], sync_after=True)
                        sc.op("vector",
                              lambda e, p=p: e.tensor_scalar(
                                  sb_sq[0:p, 2:4], sb_sq[0:p, 0:2], 1.0 / DM, None,
                                  ALU.mult),
                              reads=[b_sq], writes=[b_sq])
                        sc.op("vector",
                              lambda e, p=p: e.tensor_tensor(
                                  sb_sq[0:p, 4:5], sb_sq[0:p, 3:4], sb_sq[0:p, 3:4],
                                  ALU.mult),
                              reads=[b_sq], writes=[b_sq])
                        sc.op("vector",
                              lambda e, p=p: e.tensor_tensor(
                                  sb_sq[0:p, 5:6], sb_sq[0:p, 2:3], sb_sq[0:p, 4:5],
                                  ALU.subtract),
                              reads=[b_sq], writes=[b_sq])
                        sc.op("vector",
                              lambda e, p=p: e.tensor_scalar(
                                  sb_sq[0:p, 6:7], sb_sq[0:p, 5:6], EPS_LN, None,
                                  ALU.add),
                              reads=[b_sq], writes=[b_sq])
                        sc.op("scalar",
                              lambda e, p=p: e.activation(
                                  sb_sq[0:p, 6:7], sb_sq[0:p, 6:7], AF.Ln),
                              reads=[b_sq], writes=[b_sq])
                        sc.op("scalar",
                              lambda e, p=p: e.activation(
                                  sb_sq[0:p, 6:7], sb_sq[0:p, 6:7], AF.Exp,
                                  scale=-0.5),
                              reads=[b_sq], writes=[b_sq])
                        sc.op("vector",
                              lambda e, p=p, xl=xl, xo=xo: e.tensor_scalar(
                                  xo[0:p, :], xl[0:p, :], sb_sq[0:p, 3:4],
                                  sb_sq[0:p, 6:7], ALU.subtract, ALU.mult),
                              reads=[bx, b_sq], writes=[bo])
                        if i == 0:
                            t_x = tap("xnt0", [128, DM], BF16)
                            if t_x is not None:
                                sc.op("sync",
                                      lambda e, t_x=t_x, xo=xo: e.dma_start(
                                          out=t_x[:], in_=xo[:]),
                                      reads=[bo], dma=True)
                            t_s = tap("sq0", [128, 8], F32)
                            if t_s is not None:
                                sc.op("sync",
                                      lambda e, t_s=t_s: e.dma_start(
                                          out=t_s[:], in_=sb_sq[:]),
                                      reads=[b_sq], dma=True)
                        for j in range(NKC):
                            pb = j % 2
                            bank = banks[pb]
                            bankb = bank[:].bitcast(BF16)
                            t0 = i * 128
                            sc.op("tensor",
                                  lambda e, p=p, j=j, xo=xo, bankb=bankb: e.transpose(
                                      bankb[0:128, 0:p], xo[0:p, j * 128:(j + 1) * 128],
                                      sb_ident_b[0:p, 0:p]),
                                  reads=[bo, b_consts], writes=[b_tpb[pb]])
                            eng = "vector" if (j % 4) < 2 else "scalar"
                            def evac(e, p=p, j=j, t0=t0, bankb=bankb, eng=eng):
                                dst = sb_xnfm[:, j * TEXT + t0: j * TEXT + t0 + p]
                                if eng == "vector":
                                    return e.tensor_copy(dst, bankb[0:128, 0:p])
                                return e.copy(dst, bankb[0:128, 0:p])
                            sc.op(eng, evac, reads=[b_tpb[pb]],
                                  writes=[b_xnfm_cols[j]], deps=[])
                    sc.op("sync",
                          lambda e: e.dma_start(out=d_XNFM[:], in_=sb_xnfm[:]),
                          reads=b_xnfm_cols, writes=[b_XNFMd], dma=True)

                    t = tap("xnfm", [128, NKC * TEXT], BF16)
                    if t is not None:
                        tk = sc.op("sync",
                                   lambda e, t=t: e.dma_start(out=t[:], in_=sb_xnfm[:]),
                                   reads=b_xnfm_cols, dma=True)
                        out_toks.append(tk)

            if "ph2" in phases:
                sc.fence()
                with contextlib.ExitStack() as st2:
                    e2 = st2.enter_context
                    sb_wbuf = [e2(nc.sbuf_tensor(f"wbuf{i}", [128, NKC * 128], BF16))
                               for i in range(2)]
                    sb_sconv = e2(nc.sbuf_tensor("sconv", [128, TEXT], BF16))
                    sb_acc = e2(nc.sbuf_tensor("acc", [128, TLOC], F32))
                    sb_sx = e2(nc.sbuf_tensor("sx", [128, TLOC], BF16))
                    sb_sg = e2(nc.sbuf_tensor("sg", [128, TEXT], BF16))
                    sb_sgn = e2(nc.sbuf_tensor("sgn", [128, TLOC], BF16))
                    sb_dtf = e2(nc.sbuf_tensor("dtf", [128, TLOC], F32))
                    sb_dtt = e2(nc.sbuf_tensor("dtt", [128, TLOC], F32))
                    b_wbuf = [Buf(f"wbuf{i}") for i in range(2)]
                    b_sconv = Buf("sconv")
                    b_acc = Buf("acc")
                    b_sx = Buf("sx")
                    b_sg = Buf("sg")
                    b_sgn = Buf("sgn")
                    b_dtf = Buf("dtf")
                    b_dtt = Buf("dtt")

                    t_xbc = tap("xbc", [80, 128, TLOC], BF16)
                    t_gate = tap("gate", [64, 128, TLOC], BF16)
                    t_dt = tap("dtclip", [128, TLOC], F32)

                    NCHK = [(0, 512), (512, 512), (1024, 512), (1536, 512),
                            (2048, 3)]

                    for s in range(NSLAB):
                        wb, bw = sb_wbuf[s % 2], b_wbuf[s % 2]
                        sc.op("gpsimd",
                              lambda e, s=s, wb=wb: e.dma_start(
                                  out=wb[:], in_=d_win[s].rearrange(
                                      "k kc m -> k (kc m)")),
                              writes=[bw], dma=True)
                        for kc in range(NKC):
                            for ci, (o, n) in enumerate(NCHK):
                                sc.op("tensor",
                                      lambda e, kc=kc, o=o, n=n, ci=ci, wb=wb:
                                      e.matmul(
                                          banks[ci][:, 0:n],
                                          wb[:, kc * 128:(kc + 1) * 128],
                                          sb_xnfm[:, kc * TEXT + o:kc * TEXT + o + n],
                                          start=(kc == 0), stop=(kc == NKC - 1)),
                                      reads=[bw, b_xnfm_cols[kc]],
                                      writes=[b_bank[ci]])
                        if s < SLAB_DT:
                            for ci, (o, n) in enumerate(NCHK):
                                eng = "vector" if ci % 2 == 0 else "scalar"
                                def evc(e, ci=ci, o=o, n=n, eng=eng):
                                    if eng == "vector":
                                        return e.tensor_copy(
                                            sb_sconv[:, o:o + n], banks[ci][:, 0:n])
                                    return e.copy(
                                        sb_sconv[:, o:o + n], banks[ci][:, 0:n])
                                sc.op(eng, evc, reads=[b_bank[ci]], writes=[b_sconv])
                            if s == 0:
                                t_sc = tap("sconv0", [128, TEXT], BF16)
                                if t_sc is not None:
                                    tk = sc.op("sync",
                                               lambda e, t_sc=t_sc: e.dma_start(
                                                   out=t_sc[:], in_=sb_sconv[:]),
                                               reads=[b_sconv], dma=True)
                                    out_toks.append(tk)
                            sc.op("vector",
                                  lambda e, s=s: e.tensor_scalar(
                                      sb_acc[:], sb_sconv[:, 0:TLOC],
                                      sb_convw[:, 4 * s:4 * s + 1], None, ALU.mult),
                                  reads=[b_sconv, b_consts], writes=[b_acc],
                                  fastout=True)
                            for k in range(1, 4):
                                sc.op("vector",
                                      lambda e, s=s, k=k: e.scalar_tensor_tensor(
                                          sb_acc[:], sb_sconv[:, k:k + TLOC],
                                          sb_convw[:, 4 * s + k:4 * s + k + 1],
                                          sb_acc[:], ALU.mult, ALU.add),
                                      reads=[b_sconv, b_consts, b_acc],
                                      writes=[b_acc], fastout=True)
                            sc.op("scalar",
                                  lambda e, s=s: e.activation(
                                      sb_sx[:], sb_acc[:], AF.Silu,
                                      bias=sb_convb[:, s:s + 1]),
                                  reads=[b_acc, b_consts], writes=[b_sx],
                                  fastout=True)
                            if s < SLAB_B0:
                                dst, bd = d_XFMS[s], b_XFMS[s]
                            else:
                                dst, bd = d_BCFM[s - SLAB_B0], b_BCFM[s - SLAB_B0]
                            sc.op("sync",
                                  lambda e, dst=dst: e.dma_start(out=dst[:, :],
                                                                 in_=sb_sx[:]),
                                  reads=[b_sx], writes=[bd], dma=True)
                            if t_xbc is not None:
                                tk = sc.op("sync",
                                           lambda e, s=s: e.dma_start(
                                               out=t_xbc[s], in_=sb_sx[:]),
                                           reads=[b_sx], dma=True)
                                out_toks.append(tk)
                        elif s == SLAB_DT:
                            for ci, (o, n) in enumerate(NCHK):
                                o2 = max(o, LOOK)
                                n2 = n - (o2 - o)
                                if n2 <= 0:
                                    continue
                                sc.op("vector",
                                      lambda e, ci=ci, o=o, o2=o2, n2=n2:
                                      e.tensor_copy(
                                          sb_dtf[:, o2 - LOOK:o2 - LOOK + n2],
                                          banks[ci][:, o2 - o:o2 - o + n2]),
                                      reads=[b_bank[ci]], writes=[b_dtf])
                            sc.op("vector",
                                  lambda e: e.tensor_scalar(
                                      sb_dtf[:], sb_dtf[:], sb_dtb[:, 0:1], None,
                                      ALU.add),
                                  reads=[b_dtf, b_consts], writes=[b_dtf],
                                  fastout=True)
                            sc.op("scalar",
                                  lambda e: e.activation(sb_dtt[:], sb_dtf[:],
                                                         AF.Abs),
                                  reads=[b_dtf], writes=[b_dtt], fastout=True)
                            sc.op("scalar",
                                  lambda e: e.activation(sb_dtt[:], sb_dtt[:],
                                                         AF.Exp, scale=-1.0),
                                  reads=[b_dtt], writes=[b_dtt], fastout=True)
                            sc.op("vector",
                                  lambda e: e.tensor_scalar(
                                      sb_dtt[:], sb_dtt[:], 1.0, None, ALU.add),
                                  reads=[b_dtt], writes=[b_dtt], fastout=True)
                            sc.op("scalar",
                                  lambda e: e.activation(sb_dtt[:], sb_dtt[:],
                                                         AF.Ln),
                                  reads=[b_dtt], writes=[b_dtt], fastout=True)
                            sc.op("vector",
                                  lambda e: e.tensor_scalar(
                                      sb_dtf[:], sb_dtf[:], 0.0, None, ALU.max),
                                  reads=[b_dtf], writes=[b_dtf], fastout=True)
                            sc.op("vector",
                                  lambda e: e.tensor_tensor(
                                      sb_dtclip[:], sb_dtf[:], sb_dtt[:], ALU.add),
                                  reads=[b_dtf, b_dtt], writes=[b_dtclip],
                                  fastout=True)
                            sc.op("vector",
                                  lambda e: e.tensor_scalar(
                                      sb_dtclip[:], sb_dtclip[:], DT_MIN, DT_MAX,
                                      ALU.max, ALU.min),
                                  reads=[b_dtclip], writes=[b_dtclip],
                                  fastout=True)
                            if t_dt is not None:
                                tk = sc.op("sync",
                                           lambda e: e.dma_start(out=t_dt[:],
                                                                 in_=sb_dtclip[:]),
                                           reads=[b_dtclip], dma=True)
                                out_toks.append(tk)
                        else:
                            gi = s - SLAB_G0
                            for ci, (o, n) in enumerate(NCHK):
                                sc.op("scalar",
                                      lambda e, ci=ci, o=o, n=n: e.activation(
                                          sb_sg[:, o:o + n], banks[ci][:, 0:n],
                                          AF.Silu),
                                      reads=[b_bank[ci]], writes=[b_sg],
                                      fastout=True)
                            sc.op("vector",
                                  lambda e: e.tensor_copy(sb_sgn[:],
                                                          sb_sg[:, LOOK:TEXT]),
                                  reads=[b_sg], writes=[b_sgn], fastout=True)
                            sc.op("sync",
                                  lambda e, gi=gi: e.dma_start(out=d_GFMS[gi],
                                                               in_=sb_sgn[:]),
                                  reads=[b_sgn], writes=[b_GFMS[gi]], dma=True)
                            if t_gate is not None:
                                tk = sc.op("sync",
                                           lambda e, gi=gi: e.dma_start(
                                               out=t_gate[gi], in_=sb_sgn[:]),
                                           reads=[b_sgn], dma=True)
                                out_toks.append(tk)

        if "ph3a" in phases or "ph3" in phases:
            sc.fence()
            with contextlib.ExitStack() as st3:
                e3 = st3.enter_context
                sb_srun = e3(nc.sbuf_tensor("srun", [128, INTER], F32))
                sb_xtm = e3(nc.sbuf_tensor("xtm", [128, INTER], BF16))
                sb_gtm = e3(nc.sbuf_tensor("gtm", [128, INTER], BF16))
                sb_ytm = e3(nc.sbuf_tensor("ytm", [128, INTER], BF16))
                sb_bfm = e3(nc.sbuf_tensor("bfm", [128, 1024], BF16))
                sb_cfm = e3(nc.sbuf_tensor("cfm", [128, 1024], BF16))
                sb_btm = e3(nc.sbuf_tensor("btm", [128, 1024], BF16))
                sb_cs3 = e3(nc.sbuf_tensor("cs3", [3, CHUNK * 128], BF16))
                sb_scores = e3(nc.sbuf_tensor("scores", [128, 2048], BF16))
                sb_lmat = e3(nc.sbuf_tensor("lmat", [128, 2048], BF16))
                sb_mrep = e3(nc.sbuf_tensor("mrep", [128, 2048], BF16))
                sb_xdd = e3(nc.sbuf_tensor("xdd", [128, 1024], BF16))
                sb_xd8 = e3(nc.sbuf_tensor("xd8", [128, 1024], BF16))
                sb_srun_bf = e3(nc.sbuf_tensor("srun_bf", [128, INTER], BF16))
                sb_t1 = e3(nc.sbuf_tensor("t1", [128, 512], F32))
                sb_dfree = e3(nc.sbuf_tensor("dfree", [128, INTER], BF16))
                sb_nrep = e3(nc.sbuf_tensor("nrep", [128, INTER], BF16))
                sb_z128 = e3(nc.sbuf_tensor("z128", [128, 128], F32))
                sb_pA = e3(nc.sbuf_tensor("pA", [128, 128], F32))
                sb_csHL = e3(nc.sbuf_tensor("csHL", [128, 128], F32))
                sb_s1 = e3(nc.sbuf_tensor("s1", [128, 128], BF16))
                sb_s2 = e3(nc.sbuf_tensor("s2", [128, 128], BF16))
                sb_s3 = e3(nc.sbuf_tensor("s3", [128, 128], BF16))
                sb_r1 = e3(nc.sbuf_tensor("r1", [128, 128], F32))
                sb_r2 = e3(nc.sbuf_tensor("r2", [128, 128], F32))
                sb_lndt = e3(nc.sbuf_tensor("lndt", [128, 128], F32))
                sb_biasHL = e3(nc.sbuf_tensor("biasHL", [128, 128], F32))
                sb_biasLH = e3(nc.sbuf_tensor("biasLH", [128, 128], F32))
                sb_ddtHL = e3(nc.sbuf_tensor("ddtHL", [128, 128], F32))
                sb_ddtLH = e3(nc.sbuf_tensor("ddtLH", [128, 128], F32))
                sb_expcsLH = e3(nc.sbuf_tensor("expcsLH", [128, 128], F32))
                sb_cdbc = e3(nc.sbuf_tensor("cdbc", [128, 128], F32))
                sb_rms = e3(nc.sbuf_tensor("rms", [128, 16], F32))
                sb_flag = e3(nc.sbuf_tensor("flag", [1, 1], I32))

                d_ZERO = nc.dram_tensor("ZEROST", [128, INTER], F32,
                                        kind="Internal").ap()

                b_srun = [Buf(f"srun{h}") for h in range(NH)]
                b_xtm = Buf("xtm3")
                b_gtm = Buf("gtm3")
                b_ytm = Buf("ytm3")
                b_bfm = Buf("bfm3")
                b_cfm = Buf("cfm3")
                b_btm = Buf("btm3")
                b_cs3 = Buf("cs3")
                b_scores = Buf("scores")
                b_lmat = Buf("lmat")
                b_mrep = Buf("mrep")
                b_xdd = [Buf("xdda"), Buf("xddb")]
                b_xd8 = [Buf("xd8a"), Buf("xd8b")]
                b_srunbf = Buf("srunbf")
                b_t1 = Buf("t1")
                b_dfree = Buf("dfree")
                b_nrep = Buf("nrep")
                b_z128 = Buf("z128")
                b_prep = Buf("prep")
                b_CSR = Buf("CSRd")
                b_rms = Buf("rms")
                b_flag = Buf("flag")
                b_ZERO = Buf("ZEROd")

                hs = e3(nc.semaphore("hs"))
                hp = e3(nc.semaphore("hp"))

                GRP = 16

                def emit_prep(c, full):
                    cs = slice(c * CHUNK, (c + 1) * CHUNK)
                    src = d_XFMS.rearrange("s k t -> (s k) t")[:, cs]
                    sc.op("sync",
                          lambda e, src=src: e.dma_start(out=sb_xtm[:], in_=src,
                                                         transpose=True),
                          reads=b_XFMS, writes=[b_xtm], dma=True)
                    if full:
                        srcg = d_GFMS.rearrange("s k t -> (s k) t")[:, cs]
                        sc.op("sync",
                              lambda e, srcg=srcg: e.dma_start(
                                  out=sb_gtm[:], in_=srcg, transpose=True),
                              reads=b_GFMS, writes=[b_gtm], dma=True)
                        srcc = d_BCFM[8:16, :, cs].rearrange("g k t -> k g t")
                        sc.op("sync",
                              lambda e, srcc=srcc: e.dma_start(
                                  out=sb_cfm[:].rearrange("k (g t) -> k g t", g=8),
                                  in_=srcc),
                              reads=b_BCFM, writes=[b_cfm], dma=True)
                    srcb = d_BCFM[0:8, :, cs].rearrange("g k t -> k g t")
                    sc.op("sync",
                          lambda e, srcb=srcb: e.dma_start(
                              out=sb_bfm[:].rearrange("k (g t) -> k g t", g=8),
                              in_=srcb),
                          reads=b_BCFM, writes=[b_bfm], dma=True)
                    srcbt = d_BCFM[0:8].rearrange("g k t -> (g k) t")[:, cs]
                    sc.op("sync",
                          lambda e, srcbt=srcbt: e.dma_start(out=sb_btm[:],
                                                             in_=srcbt,
                                                             transpose=True),
                          reads=b_BCFM, writes=[b_btm], dma=True)
                    dtv = sb_dtclip[:, cs]
                    sc.op("vector",
                          lambda e, dtv=dtv: e.tensor_scalar(
                              sb_pA[:], dtv, sb_acol[:, 0:1], None, ALU.mult),
                          reads=[b_dtclip, b_consts], writes=[b_prep])
                    sc.op("vector",
                          lambda e: e.tensor_tensor_scan(
                              sb_csHL[:], sb_pA[:], sb_z128[:], 0.0, ALU.add,
                              ALU.add),
                          reads=[b_prep, b_z128], writes=[b_prep],
                          sync_after=True)
                    sc.op("vector", lambda e: e.tensor_copy(sb_s1[:], sb_csHL[:]),
                          reads=[b_prep], writes=[b_prep])
                    sc.op("vector",
                          lambda e: e.tensor_tensor(sb_r1[:], sb_csHL[:], sb_s1[:],
                                                    ALU.subtract),
                          reads=[b_prep], writes=[b_prep])
                    sc.op("vector", lambda e: e.tensor_copy(sb_s2[:], sb_r1[:]),
                          reads=[b_prep], writes=[b_prep])
                    sc.op("vector",
                          lambda e: e.tensor_tensor(sb_r2[:], sb_r1[:], sb_s2[:],
                                                    ALU.subtract),
                          reads=[b_prep], writes=[b_prep])
                    sc.op("vector", lambda e: e.tensor_copy(sb_s3[:], sb_r2[:]),
                          reads=[b_prep], writes=[b_prep])
                    for i, t in enumerate((sb_s1, sb_s2, sb_s3)):
                        sc.op("sync",
                              lambda e, i=i, t=t: e.dma_start(
                                  out=d_CSR[i].rearrange("(p n) -> p n", p=128),
                                  in_=t[:]),
                              reads=[b_prep], writes=[b_CSR], dma=True)
                    sc.op("sync",
                          lambda e: e.dma_start(out=sb_cs3[0:3, :], in_=d_CSR[:]),
                          reads=[b_CSR], writes=[b_cs3], dma=True)
                    sc.op("tensor",
                          lambda e: e.matmul(banks[5][:, 0:128], sb_ones3[:],
                                             sb_cs3[0:3, 127::128], start=True,
                                             stop=True),
                          reads=[b_cs3, b_consts], writes=[b_bank[5]])
                    sc.op("scalar",
                          lambda e: e.activation(sb_cdbc[:], banks[5][:, 0:128],
                                                 AF.Exp),
                          reads=[b_bank[5]], writes=[b_prep])
                    sc.op("vector",
                          lambda e: e.tensor_scalar(
                              sb_ddtHL[:], sb_csHL[:], sb_csHL[:, 127:128], None,
                              ALU.subtract),
                          reads=[b_prep], writes=[b_prep])
                    sc.op("scalar",
                          lambda e: e.activation(sb_ddtHL[:], sb_ddtHL[:], AF.Exp,
                                                 scale=-1.0),
                          reads=[b_prep], writes=[b_prep])
                    sc.op("vector",
                          lambda e, dtv=dtv: e.tensor_tensor(
                              sb_ddtHL[:], sb_ddtHL[:], dtv, ALU.mult),
                          reads=[b_prep, b_dtclip], writes=[b_prep])
                    sc.op("tensor",
                          lambda e: e.transpose(banks[6][:, 0:128], sb_ddtHL[:],
                                                sb_ident_f[:]),
                          reads=[b_prep, b_consts], writes=[b_bank[6]])
                    sc.op("vector",
                          lambda e: e.tensor_copy(sb_ddtLH[:], banks[6][:, 0:128]),
                          reads=[b_bank[6]], writes=[b_prep])
                    if full:
                        sc.op("scalar",
                              lambda e, dtv=dtv: e.activation(sb_lndt[:], dtv,
                                                              AF.Ln),
                              reads=[b_dtclip], writes=[b_prep])
                        sc.op("vector",
                              lambda e: e.tensor_tensor(
                                  sb_biasHL[:], sb_lndt[:], sb_csHL[:],
                                  ALU.subtract),
                              reads=[b_prep], writes=[b_prep])
                        sc.op("tensor",
                              lambda e: e.transpose(banks[6][:, 0:128],
                                                    sb_biasHL[:], sb_ident_f[:]),
                              reads=[b_prep, b_consts], writes=[b_bank[6]])
                        sc.op("vector",
                              lambda e: e.tensor_copy(sb_biasLH[:],
                                                      banks[6][:, 0:128]),
                              reads=[b_bank[6]], writes=[b_prep])
                        sc.op("tensor",
                              lambda e: e.transpose(banks[6][:, 0:128], sb_csHL[:],
                                                    sb_ident_f[:]),
                              reads=[b_prep, b_consts], writes=[b_bank[6]])
                        sc.op("scalar",
                              lambda e: e.activation(sb_expcsLH[:],
                                                     banks[6][:, 0:128], AF.Exp),
                              reads=[b_bank[6]], writes=[b_prep])

                def emit_states_batch(c, j, full):
                    h0 = 8 * j
                    xdo = (j % 2) * 512
                    for i in range(8):
                        h = h0 + i
                        sc.op("gpsimd",
                              lambda e, i=i, h=h, xdo=xdo: e.tensor_scalar(
                                  sb_xdd[:, xdo + i * 64:xdo + (i + 1) * 64],
                                  sb_xtm[:, h * 64:(h + 1) * 64],
                                  sb_ddtLH[:, h:h + 1], None, ALU.mult),
                              reads=[b_xtm, b_prep], writes=[b_xdd[j % 2]])
                    sbank = state_bank(j)
                    for i in range(8):
                        h = h0 + i
                        g = h // GRP
                        sc.op("tensor",
                              lambda e, i=i, g=g, sbank=sbank, xdo=xdo: e.matmul(
                                  banks[sbank][:, i * 64:(i + 1) * 64],
                                  sb_btm[:, g * 128:(g + 1) * 128],
                                  sb_xdd[:, xdo + i * 64:xdo + (i + 1) * 64],
                                  start=(i == 0), stop=(i == 7)),
                              reads=[b_btm, b_xdd[j % 2]], writes=[b_bank[sbank]])
                    if full:
                        for half in range(2):
                            hh = h0 + 4 * half
                            bk = half
                            sc.op("tensor",
                                  lambda e, hh=hh, bk=bk: e.matmul(
                                      banks[bk][:, 0:512], sb_ones3[:],
                                      sb_cs3[0:3, hh * 128:(hh + 4) * 128],
                                      start=True, stop=True),
                                  reads=[b_cs3, b_consts], writes=[b_bank[bk]])
                            for q in range(4):
                                h = hh + q
                                sc.op("scalar",
                                      lambda e, h=h, q=q, bk=bk: e.activation(
                                          sb_lmat[:, (h % GRP) * 128:
                                                  (h % GRP + 1) * 128],
                                          banks[bk][:, q * 128:(q + 1) * 128],
                                          AF.Exp, bias=sb_biasLH[:, h:h + 1]),
                                      reads=[b_bank[bk], b_prep],
                                      writes=[b_lmat])
                        if j % 2 == 1:
                            g = h0 // GRP
                            sc.op("tensor",
                                  lambda e, g=g: e.matmul(
                                      banks[5][:, 0:128],
                                      sb_bfm[:, g * 128:(g + 1) * 128],
                                      sb_cfm[:, g * 128:(g + 1) * 128],
                                      start=True, stop=True),
                                  reads=[b_bfm, b_cfm], writes=[b_bank[5]])
                            sc.op("vector",
                                  lambda e: e.tensor_tensor(
                                      sb_mrep[:, 0:128], banks[5][:, 0:128],
                                      sb_trimask[:], ALU.mult),
                                  reads=[b_bank[5], b_consts], writes=[b_mrep])
                            for w in (128, 256, 512, 1024):
                                sc.op("vector",
                                      lambda e, w=w: e.tensor_copy(
                                          sb_mrep[:, w:2 * w], sb_mrep[:, 0:w]),
                                      reads=[b_mrep], writes=[b_mrep],
                                      fastout=(w >= 256))
                            sc.op("vector",
                                  lambda e: e.scalar_tensor_tensor(
                                      sb_scores[:], sb_lmat[:], 3.0e38,
                                      sb_mrep[:], ALU.min, ALU.mult),
                                  reads=[b_lmat, b_mrep], writes=[b_scores],
                                  fastout=True)
                        xo = (j % 2) * 512
                        sc.op("gpsimd",
                              lambda e, h0=h0, xo=xo: e.tensor_tensor(
                                  sb_xd8[:, xo:xo + 512],
                                  sb_xtm[:, h0 * 64:(h0 + 8) * 64],
                                  sb_dfree[:, h0 * 64:(h0 + 8) * 64], ALU.mult),
                              reads=[b_xtm, b_dfree], writes=[b_xd8[j % 2]])
                    if full and j % 2 == 1:
                        for jj in (j - 1, j):
                            emit_y_batch(c, jj)
                    if not full:
                        for i in range(8):
                            h = h0 + i
                            sc.op("vector",
                                  lambda e, i=i, h=h, sbank=sbank:
                                  e.scalar_tensor_tensor(
                                      sb_srun[:, h * 64:(h + 1) * 64],
                                      sb_srun[:, h * 64:(h + 1) * 64],
                                      sb_cdbc[:, h:h + 1],
                                      banks[sbank][:, i * 64:(i + 1) * 64],
                                      ALU.mult, ALU.add),
                                  reads=[b_srun[h], b_prep, b_bank[sbank]],
                                  writes=[b_srun[h]])

                def state_bank(j):
                    return 4 if j % 2 == 0 else 7

                def emit_y_batch(c, j):
                    h0 = 8 * j
                    sbank = state_bank(j)
                    for i in range(8):
                        h = h0 + i
                        sc.op("tensor",
                              lambda e, i=i, h=h: e.matmul(
                                  banks[2][:, i * 64:(i + 1) * 64],
                                  sb_scores[:, (h % GRP) * 128:(h % GRP + 1) * 128],
                                  sb_xtm[:, h * 64:(h + 1) * 64],
                                  start=(i == 0), stop=False),
                              reads=[b_scores, b_xtm], writes=[b_bank[2]])
                    xo = (j % 2) * 512
                    sc.op("tensor",
                          lambda e, xo=xo: e.matmul(
                              banks[2][:, 0:512], sb_ident_b[:],
                              sb_xd8[:, xo:xo + 512], start=False, stop=True),
                          reads=[b_xd8[j % 2], b_consts], writes=[b_bank[2]])
                    for i in range(8):
                        h = h0 + i
                        g = h // GRP
                        sc.op("tensor",
                              lambda e, i=i, h=h, g=g: e.matmul(
                                  banks[3][:, i * 64:(i + 1) * 64],
                                  sb_cfm[:, g * 128:(g + 1) * 128],
                                  sb_srun_bf[:, h * 64:(h + 1) * 64],
                                  start=(i == 0), stop=(i == 7)),
                              reads=[b_cfm, b_srunbf], writes=[b_bank[3]])
                    for i in range(8):
                        h = h0 + i
                        sc.op("vector",
                              lambda e, i=i, h=h, sbank=sbank:
                              e.scalar_tensor_tensor(
                                  sb_srun[:, h * 64:(h + 1) * 64],
                                  sb_srun[:, h * 64:(h + 1) * 64],
                                  sb_cdbc[:, h:h + 1],
                                  banks[sbank][:, i * 64:(i + 1) * 64],
                                  ALU.mult, ALU.add),
                              reads=[b_srun[h], b_prep, b_bank[sbank]],
                              writes=[b_srun[h]])
                    e8 = sb_expcsLH[:, h0:h0 + 8].unsqueeze(2).broadcast_to(
                        [128, 8, 64])
                    sc.op("vector",
                          lambda e, e8=e8: e.tensor_tensor(
                              sb_t1[:].rearrange("p (h q) -> p h q", h=8),
                              banks[3][:].rearrange("p (h q) -> p h q", h=8),
                              e8, ALU.mult),
                          reads=[b_bank[3], b_prep], writes=[b_t1], fastout=True)
                    sc.op("vector",
                          lambda e, h0=h0: e.tensor_tensor(
                              sb_ytm[:, h0 * 64:(h0 + 8) * 64], banks[2][:],
                              sb_t1[:], ALU.add),
                          reads=[b_bank[2], b_t1], writes=[b_ytm], fastout=True)

                if "ph3a" in phases:
                    sc.op("vector", lambda e: e.memset(sb_z128[:], 0.0),
                          writes=[b_z128])
                    sc.op("vector", lambda e: e.memset(sb_srun[:], 0.0),
                          writes=list(b_srun), fastout=True)
                    sc.op("sync",
                          lambda e: e.dma_start(out=d_ZERO[:], in_=sb_srun[:]),
                          reads=list(b_srun), writes=[b_ZERO], dma=True)
                    for c in range(NCH):
                        emit_prep(c, full=False)
                        for j in range(16):
                            emit_states_batch(c, j, full=False)

                if "hand" in phases:
                    def hand_fn(e):
                        with e.register() as rpar, e.register() as rnonce, \
                                e.register() as rflag, e.register() as rcnt:
                            e.reg_load(rpar, sb_par[0:1, 0:1])
                            e.reg_load(rnonce, sb_par[0:1, 1:2])
                            e.reg_mov(rcnt, 16)
                            with e.If_eq(rpar, 0):
                                e.dma_start(out=d_SHST[:], in_=sb_srun[:]
                                            ).then_inc(hs, 16)
                                e.wait_ge(hs, 16)
                                e.store(sb_flag[0:1, 0:1], rnonce)
                                e.dma_start(out=d_SHFL[:], in_=sb_flag[:]
                                            ).then_inc(hs, 16)
                                e.wait_ge(hs, 32)
                                e.dma_start(out=sb_srun[:], in_=d_ZERO[:]
                                            ).then_inc(hs, 16)
                                e.wait_ge(hs, 48)
                            with e.Else():
                                e.reg_mov(rflag, 0)

                                def cond():
                                    e.dma_start(out=sb_flag[:], in_=d_SHFL[:]
                                                ).then_inc(hp, 16)
                                    e.wait_ge(hp, rcnt)
                                    e.reg_add(rcnt, rcnt, 16)
                                    e.reg_load(rflag, sb_flag[0:1, 0:1])
                                    return (e.compute_val(rflag) !=
                                            e.compute_val(rnonce))

                                with e.While(cond):
                                    e.nop(cycle_cnt=2000)
                                e.dma_start(out=sb_srun[:], in_=d_SHST[:]
                                            ).then_inc(hs, 16)
                                e.wait_ge(hs, 16)
                        return e.nop()
                    sc.op("sync", hand_fn,
                          reads=list(b_srun) + [b_consts, b_ZERO],
                          writes=list(b_srun) + [b_flag])

                if "ph3" in phases:
                    if "ph3a" not in phases:
                        sc.op("vector", lambda e: e.memset(sb_z128[:], 0.0),
                              writes=[b_z128])
                        sc.op("vector", lambda e: e.memset(sb_srun[:], 0.0),
                              writes=list(b_srun), fastout=True)
                    sc.op("gpsimd",
                          lambda e: e.dma_start(out=sb_cs3[0:1, 0:INTER],
                                                in_=d_drep[:]),
                          writes=[b_cs3], dma=True)
                    for q in range(16):
                        sc.op("tensor",
                              lambda e, q=q: e.matmul(
                                  banks[q % 2][:, 0:512],
                                  sb_ones3[0:1, :],
                                  sb_cs3[0:1, q * 512:(q + 1) * 512],
                                  start=True, stop=True),
                              reads=[b_cs3, b_consts], writes=[b_bank[q % 2]])
                        sc.op("vector",
                              lambda e, q=q: e.tensor_copy(
                                  sb_dfree[:, q * 512:(q + 1) * 512],
                                  banks[q % 2][:, 0:512]),
                              reads=[b_bank[q % 2]], writes=[b_dfree],
                              fastout=True)
                    sc.op("gpsimd",
                          lambda e: e.dma_start(out=sb_cs3[0:1, 0:INTER],
                                                in_=d_nrep[:]),
                          reads=[b_cs3], writes=[b_cs3], dma=True)
                    for q in range(16):
                        sc.op("tensor",
                              lambda e, q=q: e.matmul(
                                  banks[q % 2][:, 0:512],
                                  sb_ones3[0:1, :],
                                  sb_cs3[0:1, q * 512:(q + 1) * 512],
                                  start=True, stop=True),
                              reads=[b_cs3, b_consts], writes=[b_bank[q % 2]])
                        sc.op("vector",
                              lambda e, q=q: e.tensor_copy(
                                  sb_nrep[:, q * 512:(q + 1) * 512],
                                  banks[q % 2][:, 0:512]),
                              reads=[b_bank[q % 2]], writes=[b_nrep],
                              fastout=True)
                    t_y = tap("ytm", [NCH, 128, INTER], BF16)
                    t_yn = tap("yntm", [NCH, 128, INTER], BF16)
                    for c in range(NCH):
                        emit_prep(c, full=True)
                        sc.op("vector",
                              lambda e: e.tensor_copy(sb_srun_bf[:], sb_srun[:]),
                              reads=list(b_srun), writes=[b_srunbf],
                              fastout=True)
                        for j in range(16):
                            emit_states_batch(c, j, full=True)
                        if t_y is not None:
                            tk = sc.op("sync",
                                       lambda e, c=c, t_y=t_y: e.dma_start(
                                           out=t_y[c], in_=sb_ytm[:]),
                                       reads=[b_ytm], dma=True)
                            out_toks.append(tk)
                        sc.op("vector",
                              lambda e: e.tensor_tensor(sb_xtm[:], sb_ytm[:],
                                                        sb_gtm[:], ALU.mult),
                              reads=[b_ytm, b_gtm], writes=[b_xtm],
                              fastout=True)
                        for g in range(NG):
                            sc.op("scalar",
                                  lambda e, g=g: e.activation(
                                      sb_scores[:, 0:1024],
                                      sb_xtm[:, g * 1024:(g + 1) * 1024],
                                      AF.Square, accum_out=sb_rms[:, g:g + 1]),
                                  reads=[b_xtm], writes=[b_scores, b_rms],
                                  sync_after=True)
                        sc.op("vector",
                              lambda e: e.tensor_scalar(
                                  sb_rms[:, 8:16], sb_rms[:, 0:8], 1.0 / 1024.0,
                                  EPS_RMS, ALU.mult, ALU.add),
                              reads=[b_rms], writes=[b_rms])
                        sc.op("scalar",
                              lambda e: e.activation(sb_rms[:, 8:16],
                                                     sb_rms[:, 8:16], AF.Ln),
                              reads=[b_rms], writes=[b_rms])
                        sc.op("scalar",
                              lambda e: e.activation(sb_rms[:, 8:16],
                                                     sb_rms[:, 8:16], AF.Exp,
                                                     scale=-0.5),
                              reads=[b_rms], writes=[b_rms])
                        for g in range(NG):
                            sc.op("vector",
                                  lambda e, g=g: e.tensor_scalar(
                                      sb_gtm[:, g * 1024:(g + 1) * 1024],
                                      sb_xtm[:, g * 1024:(g + 1) * 1024],
                                      sb_rms[:, 8 + g:9 + g], None, ALU.mult),
                                  reads=[b_xtm, b_rms], writes=[b_gtm],
                                  fastout=True)
                        sc.op("vector",
                              lambda e: e.tensor_tensor(sb_ytm[:], sb_gtm[:],
                                                        sb_nrep[:], ALU.mult),
                              reads=[b_gtm, b_nrep], writes=[b_ytm],
                              fastout=True)
                        sc.op("sync",
                              lambda e, c=c: e.dma_start(out=d_YNTM[c],
                                                         in_=sb_ytm[:]),
                              reads=[b_ytm], writes=[b_YNTM[c]], dma=True)
                        if t_yn is not None:
                            tk = sc.op("sync",
                                       lambda e, c=c, t_yn=t_yn: e.dma_start(
                                           out=t_yn[c], in_=sb_ytm[:]),
                                       reads=[b_ytm], dma=True)
                            out_toks.append(tk)

        if "ph4" in phases:
            sc.fence()
            with contextlib.ExitStack() as st4:
                e4 = st4.enter_context
                sb_yn = e4(nc.sbuf_tensor("yn", [128, 64 * 1024], BF16))
                sb_wo = [e4(nc.sbuf_tensor(f"wo{i}", [128, 64 * 128], BF16))
                         for i in range(2)]
                sb_so = [e4(nc.sbuf_tensor(f"so{i}", [128, 1024], BF16))
                        for i in range(2)]
                b_yn = Buf("yn")
                b_wo = [Buf(f"wo{i}") for i in range(2)]
                b_so = [Buf(f"so{i}") for i in range(2)]
                t_sfm = tap("sfm", [32, 128, TLOC], BF16)

                for half in range(2):
                    for kc in range(64):
                        src = d_YNTM[8 * half:8 * half + 8, :,
                                     kc * 128:(kc + 1) * 128].rearrange(
                                         "c t f -> (c t) f")
                        sc.op("sync",
                              lambda e, src=src, kc=kc: e.dma_start(
                                  out=sb_yn[:, kc * 1024:(kc + 1) * 1024],
                                  in_=src, transpose=True),
                              reads=b_YNTM, writes=[b_yn], dma=True)
                    for mc in range(32):
                        wb, bw = sb_wo[mc % 2], b_wo[mc % 2]
                        sc.op("gpsimd",
                              lambda e, mc=mc, wb=wb: e.dma_start(
                                  out=wb[:],
                                  in_=d_wout[mc].rearrange("k kc m -> k (kc m)")),
                              writes=[bw], dma=True)
                        for kc in range(64):
                            for ci in range(2):
                                sc.op("tensor",
                                      lambda e, kc=kc, ci=ci, wb=wb: e.matmul(
                                          banks[ci][:, 0:512],
                                          wb[:, kc * 128:(kc + 1) * 128],
                                          sb_yn[:, kc * 1024 + ci * 512:
                                                kc * 1024 + ci * 512 + 512],
                                          start=(kc == 0), stop=(kc == 63)),
                                      reads=[bw, b_yn], writes=[b_bank[ci]])
                        so, bso = sb_so[mc % 2], b_so[mc % 2]
                        for ci in range(2):
                            eng = "vector" if ci == 0 else "scalar"
                            def evo(e, ci=ci, so=so, eng=eng):
                                if eng == "vector":
                                    return e.tensor_copy(
                                        so[:, ci * 512:(ci + 1) * 512],
                                        banks[ci][:, 0:512])
                                return e.copy(so[:, ci * 512:(ci + 1) * 512],
                                              banks[ci][:, 0:512])
                            sc.op(eng, evo, reads=[b_bank[ci]], writes=[bso])
                        sc.op("sync",
                              lambda e, mc=mc, half=half, so=so: e.dma_start(
                                  out=d_SFM[mc][:, half * 1024:(half + 1) * 1024],
                                  in_=so[:]),
                              reads=[bso], writes=[b_SFM[mc]], dma=True)
                        if t_sfm is not None:
                            tk = sc.op("sync",
                                       lambda e, mc=mc, half=half, so=so:
                                       e.dma_start(
                                           out=t_sfm[mc, :,
                                                     half * 1024:(half + 1) * 1024],
                                           in_=so[:]),
                                       reads=[bso], dma=True)
                            out_toks.append(tk)

        if "ph5" in phases:
            sc.fence()
            with contextlib.ExitStack() as st5:
                e5 = st5.enter_context
                sb_xn5 = e5(nc.sbuf_tensor("xn5", [128, 32 * 1024], BF16))
                sb_wb5 = [e5(nc.sbuf_tensor(f"wb5{i}", [128, 4096], BF16))
                          for i in range(2)]
                sb_A5 = e5(nc.sbuf_tensor("A5", [128, 4096], BF16))
                sb_B5 = e5(nc.sbuf_tensor("B5", [128, 4096], BF16))
                sb_wr5 = e5(nc.sbuf_tensor("wr5", [128, 256], BF16))
                sb_lg = e5(nc.sbuf_tensor("lg", [8, 1024], F32))
                sb_tp = e5(nc.sbuf_tensor("tp", [128, 48], F32))
                sb_wtm = e5(nc.sbuf_tensor("wtm", [128, 8], F32))
                sb_wfm = e5(nc.sbuf_tensor("wfm", [8, 1024], F32))
                sb_w8 = e5(nc.sbuf_tensor("w8", [128, 1024], F32))
                sb_mid = e5(nc.sbuf_tensor("mid", [128, 1024], BF16))
                sb_s5 = e5(nc.sbuf_tensor("s5", [128, 1024], BF16))
                sb_x5 = e5(nc.sbuf_tensor("x5", [128, 1024], F32))
                sb_g5 = e5(nc.sbuf_tensor("g5", [128, 1024], BF16))
                sb_o5 = e5(nc.sbuf_tensor("o5", [128, 1024], F32))
                b_xn5 = Buf("xn5")
                b_wb5 = [Buf(f"wb5{i}") for i in range(2)]
                b_A5, b_B5, b_wr5 = Buf("A5"), Buf("B5"), Buf("wr5")
                b_lg, b_tp, b_wtm, b_wfm = Buf("lg"), Buf("tp"), Buf("wtm"), Buf("wfm")
                b_w8, b_mid = Buf("w8"), Buf("mid")
                b_s5, b_x5, b_g5, b_o5 = Buf("s5"), Buf("x5"), Buf("g5"), Buf("o5")

                sc.op("gpsimd",
                      lambda e: e.dma_start(
                          out=sb_A5[:],
                          in_=d_loraA.rearrange("k kc j -> k (kc j)")),
                      writes=[b_A5], dma=True)
                sc.op("gpsimd",
                      lambda e: e.dma_start(
                          out=sb_B5[:],
                          in_=d_loraB.rearrange("j mc m -> j (mc m)")),
                      writes=[b_B5], dma=True)
                sc.op("gpsimd",
                      lambda e: e.dma_start(out=sb_wr5[:], in_=d_wrout[:]),
                      writes=[b_wr5], dma=True)

                t_w = tap("wdense", [2, 8, 1024], F32)
                t_ffn = tap("gsilu", [32, 128, TLOC], BF16)

                for half in range(2):
                    src = d_XNFM[:].rearrange("p (kc t) -> p kc t", kc=NKC)[
                        :, :, LOOK + half * 1024: LOOK + (half + 1) * 1024]
                    sc.op("sync",
                          lambda e, src=src: e.dma_start(
                              out=sb_xn5[:].rearrange("p (kc t) -> p kc t",
                                                      kc=NKC),
                              in_=src),
                          reads=[b_XNFMd], writes=[b_xn5], dma=True)
                    for kc in range(NKC):
                        for ci in range(2):
                            sc.op("tensor",
                                  lambda e, kc=kc, ci=ci: e.matmul(
                                      banks[6 + ci][0:8, 0:512],
                                      sb_wr5[:, kc * 8:(kc + 1) * 8],
                                      sb_xn5[:, kc * 1024 + ci * 512:
                                             kc * 1024 + ci * 512 + 512],
                                      start=(kc == 0), stop=(kc == NKC - 1)),
                                  reads=[b_wr5, b_xn5], writes=[b_bank[6 + ci]])
                    for ci in range(2):
                        sc.op("vector",
                              lambda e, ci=ci: e.tensor_copy(
                                  sb_lg[:, ci * 512:(ci + 1) * 512],
                                  banks[6 + ci][0:8, 0:512]),
                              reads=[b_bank[6 + ci]], writes=[b_lg])
                    for tc in range(8):
                        lgt = sb_lg[:, tc * 128:(tc + 1) * 128]
                        sc.op("tensor",
                              lambda e, lgt=lgt: e.transpose(
                                  banks[2][0:128, 0:8], lgt,
                                  sb_ident_f[0:8, 0:8]),
                              reads=[b_lg, b_consts], writes=[b_bank[2]])
                        sc.op("vector",
                              lambda e: e.tensor_copy(sb_tp[:, 0:8],
                                                      banks[2][0:128, 0:8]),
                              reads=[b_bank[2]], writes=[b_tp])
                        sc.op("vector",
                              lambda e: e.reduce_max(sb_tp[:, 8:9],
                                                     sb_tp[:, 0:8], axis=AX.X),
                              reads=[b_tp], writes=[b_tp], sync_after=True)
                        sc.op("vector",
                              lambda e: e.tensor_scalar(
                                  sb_tp[:, 16:24], sb_tp[:, 0:8],
                                  sb_tp[:, 8:9], None, ALU.subtract),
                              reads=[b_tp], writes=[b_tp])
                        sc.op("vector",
                              lambda e: e.tensor_scalar(
                                  sb_tp[:, 24:32], sb_tp[:, 16:24], 0.0, -1.0e30,
                                  ALU.is_ge, ALU.mult),
                              reads=[b_tp], writes=[b_tp])
                        sc.op("vector",
                              lambda e: e.tensor_tensor(
                                  sb_tp[:, 24:32], sb_tp[:, 16:24],
                                  sb_tp[:, 24:32], ALU.add),
                              reads=[b_tp], writes=[b_tp])
                        sc.op("vector",
                              lambda e: e.reduce_max(sb_tp[:, 9:10],
                                                     sb_tp[:, 24:32], axis=AX.X),
                              reads=[b_tp], writes=[b_tp], sync_after=True)
                        sc.op("scalar",
                              lambda e: e.activation(sb_tp[:, 32:40],
                                                     sb_tp[:, 16:24], AF.Exp),
                              reads=[b_tp], writes=[b_tp])
                        sc.op("vector",
                              lambda e: e.tensor_scalar(
                                  sb_tp[:, 24:32], sb_tp[:, 16:24],
                                  sb_tp[:, 9:10], None, ALU.is_ge),
                              reads=[b_tp], writes=[b_tp])
                        sc.op("scalar",
                              lambda e: e.activation(sb_tp[:, 10:11],
                                                     sb_tp[:, 9:10], AF.Exp),
                              reads=[b_tp], writes=[b_tp])
                        sc.op("vector",
                              lambda e: e.tensor_scalar(
                                  sb_tp[:, 10:11], sb_tp[:, 10:11], 1.0, None,
                                  ALU.add),
                              reads=[b_tp], writes=[b_tp])
                        sc.op("vector",
                              lambda e: e.reciprocal(sb_tp[:, 11:12],
                                                     sb_tp[:, 10:11]),
                              reads=[b_tp], writes=[b_tp])
                        sc.op("vector",
                              lambda e: e.tensor_scalar(
                                  sb_tp[:, 11:12], sb_tp[:, 11:12],
                                  LORA_SCALE, None, ALU.mult),
                              reads=[b_tp], writes=[b_tp])
                        sc.op("vector",
                              lambda e: e.scalar_tensor_tensor(
                                  sb_wtm[:, 0:8], sb_tp[:, 32:40],
                                  sb_tp[:, 11:12], sb_tp[:, 24:32],
                                  ALU.mult, ALU.mult),
                              reads=[b_tp], writes=[b_wtm])
                        sc.op("tensor",
                              lambda e: e.transpose(banks[2][0:8, 0:128],
                                                    sb_wtm[:, 0:8],
                                                    sb_ident_f[:]),
                              reads=[b_wtm, b_consts], writes=[b_bank[2]])
                        sc.op("vector",
                              lambda e, tc=tc: e.tensor_copy(
                                  sb_wfm[:, tc * 128:(tc + 1) * 128],
                                  banks[2][0:8, 0:128]),
                              reads=[b_bank[2]], writes=[b_wfm])
                    if t_w is not None:
                        tk = sc.op("sync",
                                   lambda e, half=half, t_w=t_w: e.dma_start(
                                       out=t_w[half], in_=sb_wfm[:]),
                                   reads=[b_wfm], dma=True)
                        out_toks.append(tk)
                    for kc in range(NKC):
                        for ci in range(2):
                            sc.op("tensor",
                                  lambda e, kc=kc, ci=ci: e.matmul(
                                      banks[4 + ci][:, 0:512],
                                      sb_A5[:, kc * 128:(kc + 1) * 128],
                                      sb_xn5[:, kc * 1024 + ci * 512:
                                             kc * 1024 + ci * 512 + 512],
                                      start=(kc == 0), stop=(kc == NKC - 1)),
                                  reads=[b_A5, b_xn5], writes=[b_bank[4 + ci]])
                    for ci in range(2):
                        sc.op("tensor",
                              lambda e, ci=ci: e.matmul(
                                  banks[2 + ci][:, 0:512], sb_eexp[:],
                                  sb_wfm[0:8, ci * 512:(ci + 1) * 512],
                                  start=True, stop=True),
                              reads=[b_wfm, b_consts], writes=[b_bank[2 + ci]])
                        sc.op("vector",
                              lambda e, ci=ci: e.tensor_copy(
                                  sb_w8[:, ci * 512:(ci + 1) * 512],
                                  banks[2 + ci][:, 0:512]),
                              reads=[b_bank[2 + ci]], writes=[b_w8])
                        sc.op("vector",
                              lambda e, ci=ci: e.tensor_tensor(
                                  sb_mid[:, ci * 512:(ci + 1) * 512],
                                  banks[4 + ci][:, 0:512],
                                  sb_w8[:, ci * 512:(ci + 1) * 512], ALU.mult),
                              reads=[b_bank[4 + ci], b_w8], writes=[b_mid])
                    for mc in range(32):
                        wb, bw = sb_wb5[mc % 2], b_wb5[mc % 2]
                        sc.op("gpsimd",
                              lambda e, mc=mc, wb=wb: e.dma_start(
                                  out=wb[:],
                                  in_=d_wbase[mc].rearrange("k kc m -> k (kc m)")),
                              writes=[bw], dma=True)
                        sc.op("sync",
                              lambda e, mc=mc, half=half: e.dma_start(
                                  out=sb_s5[:],
                                  in_=d_SFM[mc][:, half * 1024:(half + 1) * 1024]),
                              reads=[b_SFM[mc]], writes=[b_s5], dma=True)
                        sc.op("sync",
                              lambda e, mc=mc, half=half: e.dma_start(
                                  out=sb_x5[:],
                                  in_=d_xfm[mc][:, half * 1024:(half + 1) * 1024]),
                              writes=[b_x5], dma=True)
                        for ci in range(2):
                            for kc in range(NKC):
                                sc.op("tensor",
                                      lambda e, kc=kc, ci=ci, wb=wb: e.matmul(
                                          banks[ci][:, 0:512],
                                          wb[:, kc * 128:(kc + 1) * 128],
                                          sb_xn5[:, kc * 1024 + ci * 512:
                                                 kc * 1024 + ci * 512 + 512],
                                          start=(kc == 0), stop=False),
                                      reads=[bw, b_xn5], writes=[b_bank[ci]])
                            sc.op("tensor",
                                  lambda e, mc=mc, ci=ci: e.matmul(
                                      banks[ci][:, 0:512],
                                      sb_B5[:, mc * 128:(mc + 1) * 128],
                                      sb_mid[:, ci * 512:(ci + 1) * 512],
                                      start=False, stop=True),
                                  reads=[b_B5, b_mid], writes=[b_bank[ci]])
                            sc.op("scalar",
                                  lambda e, ci=ci: e.activation(
                                      sb_g5[:, ci * 512:(ci + 1) * 512],
                                      banks[ci][:, 0:512], AF.Silu),
                                  reads=[b_bank[ci]], writes=[b_g5])
                        sc.op("vector",
                              lambda e: e.tensor_tensor(
                                  sb_o5[:], sb_s5[:], sb_g5[:], ALU.mult),
                              reads=[b_s5, b_g5], writes=[b_o5])
                        sc.op("vector",
                              lambda e: e.tensor_tensor(
                                  sb_o5[:], sb_x5[:], sb_o5[:], ALU.add),
                              reads=[b_x5, b_o5], writes=[b_o5])
                        tk = sc.op("sync",
                                   lambda e, mc=mc, half=half: e.dma_start(
                                       out=d_out[mc][:, half * 1024:
                                                     (half + 1) * 1024],
                                       in_=sb_o5[:]),
                                   reads=[b_o5], dma=True)
                        out_toks.append(tk)
                        if t_ffn is not None:
                            tk = sc.op("sync",
                                       lambda e, mc=mc, half=half: e.dma_start(
                                           out=t_ffn[mc, :, half * 1024:
                                                     (half + 1) * 1024],
                                           in_=sb_g5[:]),
                                       reads=[b_g5], dma=True)
                            out_toks.append(tk)

        sc.op("sync", lambda e: e.nop(), deps=out_toks + sc.barrier_toks())

        block = ent(nc.Block())
        sc.emit(block, sems)

    return nc, tap_handles



def _prep_shared(inputs):
    Win = np.asarray(inputs["L0_attn_in_proj_w"])
    Wout = np.asarray(inputs["L0_attn_out_proj_w"])
    Wb = np.asarray(inputs["L0_ffn_base_W"])
    Wr = np.asarray(inputs["L0_ffn_router_W"])
    lA = np.asarray(inputs["L0_ffn_lora_A"])
    lB = np.asarray(inputs["L0_ffn_lora_B"])
    convw = np.asarray(inputs["L0_attn_conv_w"])
    convb = np.asarray(inputs["L0_attn_conv_b"])
    dtb = np.asarray(inputs["L0_attn_dt_bias"])
    Dv = np.asarray(inputs["L0_attn_D"])
    normw = np.asarray(inputs["L0_attn_norm_w"])

    perm = np.concatenate([
        np.arange(INTER, INTER + INTER),
        np.arange(INTER + INTER, INTER + INTER + 1024),
        np.arange(INTER + INTER + 1024, INTER + CONV_DIM),
        np.arange(INTER + CONV_DIM, PROJ),
        np.arange(0, INTER),
    ])
    Winp = Win[perm]
    w_in = np.ascontiguousarray(
        Winp.reshape(NSLAB, 128, NKC, 128).transpose(0, 3, 2, 1))

    w_out = np.ascontiguousarray(
        Wout.reshape(32, 128, 64, 128).transpose(0, 3, 2, 1))
    w_base = np.ascontiguousarray(
        Wb.reshape(32, 128, 32, 128).transpose(0, 3, 2, 1))
    w_router = np.ascontiguousarray(
        Wr.reshape(8, 32, 128).transpose(2, 1, 0).reshape(128, 256))
    loraA = np.ascontiguousarray(
        lA.transpose(1, 0, 2).reshape(NKC, 128, 8 * 16).transpose(1, 0, 2))
    loraB = np.ascontiguousarray(
        lB.reshape(128, 32, 128))

    cw = convw[:, 0, :].reshape(80, 128, 4).transpose(1, 0, 2).reshape(128, 320)
    cb = convb.reshape(80, 128).T
    a_col = (-np.arange(1, NH + 1, dtype=np.float32)).reshape(128, 1)
    d_rep = np.repeat(Dv.astype(np.float32), HD).reshape(1, INTER)
    normw_g = np.ascontiguousarray(normw.reshape(64, 128).T)

    ident_b = np.eye(128, dtype=np.float32).astype(ml_dtypes.bfloat16)
    ident_f = np.eye(128, dtype=np.float32)
    trimask = (np.arange(128)[:, None] <= np.arange(128)[None, :]).astype(np.float32)
    e_expand = (np.arange(128)[None, :] // 16 == np.arange(8)[:, None]).astype(
        np.float32)
    ones3 = np.ones((3, 128), dtype=np.float32).astype(ml_dtypes.bfloat16)

    return dict(w_in=w_in, w_out=w_out, w_base=w_base, w_router=w_router,
                loraA=loraA, loraB=loraB, convw=np.ascontiguousarray(cw),
                convb=np.ascontiguousarray(cb),
                dt_bias=dtb.reshape(128, 1).astype(np.float32),
                a_col=a_col, d_rep=d_rep, normw_g=normw_g,
                normw_rep=normw.astype(np.float32).reshape(1, INTER),
                ident_b=ident_b, ident_f=ident_f, trimask=trimask,
                e_expand=e_expand, ones3=ones3)


def make_in_maps(inputs):
    x = np.asarray(inputs["L0_attn_hidden_states"])
    shared = _prep_shared(inputs)
    nonce = int(np.random.randint(1, 2**30))
    maps = []
    for c in range(N_CORES):
        b, half = c // 2, c % 2
        t0 = half * TLOC
        xln = np.empty((TEXT, DM), dtype=np.float32)
        if half == 0:
            xln[0:LOOK] = 0.0
        else:
            xln[0:LOOK] = x[b, t0 - LOOK:t0]
        xln[LOOK:] = x[b, t0:t0 + TLOC]
        xfm = np.ascontiguousarray(
            x[b, t0:t0 + TLOC].T.reshape(NKC, 128, TLOC))
        par = np.array([[half, nonce, 0, 0]], dtype=np.int32)
        m = dict(shared)
        m.update(xln=xln, xfm=xfm, par=par)
        maps.append(m)
    return maps


def assemble(results):
    out = np.empty((B, S, DM), dtype=np.float32)
    for c in range(N_CORES):
        b, half = c // 2, c % 2
        t0 = half * TLOC
        o = np.asarray(results[c]["out"])
        out[b, t0:t0 + TLOC, :] = o.reshape(DM, TLOC).T
    return out


def kernel(**inputs):
    maps = make_in_maps(inputs)
    nc, _ = build_program()
    res = run_bass_kernel_spmd(nc, maps, core_ids=list(range(N_CORES)))
    return assemble(res.results)
